# version 11
# baseline (speedup 1.0000x reference)
import numpy as np
import ml_dtypes
import concourse.bass as bass
import concourse.mybir as mybir
from concourse.bass_utils import run_bass_kernel_spmd

F32 = mybir.dt.float32
BF16 = mybir.dt.bfloat16
AL = mybir.AluOpType
AF = mybir.ActivationFunctionType
AX = mybir.AxisListType

E = 2048
NH = 16
DH = 128
O3 = 6144
II = 8192
CH = 512
NE = E // 128
NI = II // 128

MAGIC = 12582912.0
NEGM = -1e30
SM_SCALE = 1.0 / float(np.sqrt(128.0))
NRM = float(np.float32(1.0) / np.float32(np.sqrt(8192.0)))
EPS = 1e-5
HALO = 512


def _had(n):
    i = np.arange(n)
    pc = np.array([bin(v).count("1") for v in range(n)])
    m = pc[i[:, None] & i[None, :]]
    return np.where(m % 2 == 0, 1.0, -1.0).astype(np.float32)


class Prog:

    def __init__(self, nc):
        self.nc = nc
        self.cm = []
        self.sem = {}
        self.cnt = {}
        self.dmaeng = {}

    def enter(self, cm):
        v = cm.__enter__()
        self.cm.append(cm)
        return v

    def close(self):
        for cm in reversed(self.cm):
            cm.__exit__(None, None, None)
        self.cm = []

    def mksem(self, *names):
        for n in names:
            self.sem[n] = self.enter(self.nc.semaphore(n))
            self.cnt[n] = 0

    def mkdma(self, eng, *names):
        for n in names:
            self.mksem(n)
            self.dmaeng[n] = eng

    def dma(self, eng, dst, src, s, deps=()):
        self.wdeps(eng, deps)
        eng.dma_start(dst, src).then_inc(self.sem[s], 16)
        self.cnt[s] += 16
        return (s, self.cnt[s])

    def dmat(self, eng, dst, src, s, deps=()):
        self.wdeps(eng, deps)
        eng.dma_start_transpose(dst, src).then_inc(self.sem[s], 16)
        self.cnt[s] += 16
        return (s, self.cnt[s])

    def tick(self, instr, s, k=1):
        instr.then_inc(self.sem[s], k)
        self.cnt[s] += k
        return (s, self.cnt[s])

    def mark(self, s):
        return (s, self.cnt[s])

    def wdeps(self, eng, deps):
        for d in deps:
            if not d:
                continue
            (s, v) = d
            if v > 0:
                eng.wait_ge(self.sem[s], v)

    def barrier(self):
        for s, eng in self.dmaeng.items():
            if self.cnt[s] > 0:
                eng.wait_ge(self.sem[s], self.cnt[s])
        self.nc.all_engine_barrier()


def build_nc(T=2048, dbg=False):
    TH = T + HALO
    NCH = T // CH
    nc = bass.Bass("TRN2", target_bir_lowering=False)

    xc = nc.dram_tensor("xc", [TH, E], F32, kind="ExternalInput").ap()
    w_qkv = nc.dram_tensor("w_qkv", [O3, E], F32, kind="ExternalInput").ap()
    w_out = nc.dram_tensor("w_out", [E, E], F32, kind="ExternalInput").ap()
    w_gate = nc.dram_tensor("w_gate", [II, E], F32, kind="ExternalInput").ap()
    w_up = nc.dram_tensor("w_up", [II, E], F32, kind="ExternalInput").ap()
    w_down = nc.dram_tensor("w_down", [E, II], F32, kind="ExternalInput").ap()
    maskz = nc.dram_tensor("maskz", [4, 128, 640], F32, kind="ExternalInput").ap()
    out_c = nc.dram_tensor("out_c", [T, E], F32, kind="ExternalOutput").ap()

    wqkv_bf = nc.dram_tensor("wqkv_bf", [O3, E], BF16).ap()
    wout_bf = nc.dram_tensor("wout_bf", [E, E], BF16).ap()
    tern_g = nc.dram_tensor("tern_g", [II, E], BF16).ap()
    tern_u = nc.dram_tensor("tern_u", [II, E], BF16).ap()
    wds = nc.dram_tensor("wds", [E, II], BF16).ap()
    xnb_s = nc.dram_tensor("xnb_s", [TH, E], BF16).ap()
    xnq_s = nc.dram_tensor("xnq_s", [T, E], BF16).ap()
    q_s = nc.dram_tensor("q_s", [E, TH], BF16).ap()
    k_s = nc.dram_tensor("k_s", [E, TH], BF16).ap()
    vt_s = nc.dram_tensor("vt_s", [TH, E], BF16).ap()
    att_s = nc.dram_tensor("att_s", [E, T], BF16).ap()
    inter_s = nc.dram_tensor("inter_s", [II, T], BF16).ap()
    if dbg:
        dbg_yq = nc.dram_tensor("dbg_yq", [128, II * (T // max(1, T // 1024)) // 128], F32).ap()
        dbg_h = nc.dram_tensor("dbg_h", [128, II * (T // max(1, T // 1024)) // 128], F32).ap()
        dbg_m = nc.dram_tensor("dbg_m", [128, 4 * (T // max(1, T // 1024))], F32).ap()

    h128_d = nc.inline_tensor(_had(128).astype(ml_dtypes.bfloat16), "h128c")
    h64 = _had(64)
    bdm = np.zeros((128, 128), np.float32)
    bdm[:64, :64] = h64
    bdm[64:, 64:] = h64
    bd64_d = nc.inline_tensor(bdm.astype(ml_dtypes.bfloat16), "bd64c")
    idnb_d = nc.inline_tensor(np.eye(128, dtype=ml_dtypes.bfloat16), "idnbc")
    idnf_d = nc.inline_tensor(np.eye(128, dtype=np.float32), "idnfc")
    ones_d = nc.inline_tensor(np.ones((1, 128), np.float32), "ones1c")
    pp = np.arange(128)[:, None]
    jl = np.arange(128)[None, :]
    pp2 = np.arange(128)[:, None]
    jl2 = np.arange(640)[None, :]
    mb = np.zeros((128, 256), np.float32)
    mb[:, 0:128] = np.where(jl >= pp, 0.0, NEGM)
    mb[:, 128:256] = np.where(jl <= pp, 0.0, NEGM)
    maskb_d = nc.inline_tensor(mb, "maskbc")
    mg = np.zeros((128, 4 * 640), np.float32)
    for s_ in range(4):
        ok = (jl2 >= pp2) & (jl2 <= pp2 + 512)
        mg[:, s_ * 640:(s_ + 1) * 640] = np.where(ok, 0.0, NEGM * SM_SCALE)
    maskg_d = nc.inline_tensor(mg.astype(np.float32), "maskgc")
    cst_d = nc.inline_tensor(
        np.tile(np.array([[MAGIC, EPS]], np.float32), (128, 1)), "cstc")

    P = Prog(nc)
    P.mksem("te", "ve", "ae")
    P.mkdma(nc.sync, "cs", "xs", "ws", "wt", "wu", "so", "sp")
    P.mkdma(nc.scalar, "la", "lb", "ka", "kb", "xa", "xb", "wa", "wb")
    P.mkdma(nc.gpsimd, "wc", "sa", "sb")
    sb = P.enter

    idnb = sb(nc.sbuf_tensor("idnb", [128, 128], BF16))
    idnf = sb(nc.sbuf_tensor("idnf", [128, 128], F32))
    h128 = sb(nc.sbuf_tensor("h128", [128, 128], BF16))
    bd64 = sb(nc.sbuf_tensor("bd64", [128, 128], BF16))
    ones1 = sb(nc.sbuf_tensor("ones1", [1, 128], F32))
    maskb_t = sb(nc.sbuf_tensor("maskb_t", [128, 256], F32))

    acc = sb(nc.sbuf_tensor("acc", [128, 3], F32))
    mbar = sb(nc.sbuf_tensor("mbar", [128, 3], F32))
    swi = sb(nc.sbuf_tensor("swi", [128, 3], F32))
    cst = sb(nc.sbuf_tensor("cst", [128, 2], F32))
    nc.const_aps.register(F32, MAGIC, cst[:, 0:1])
    nc.const_aps.register(F32, EPS, cst[:, 1:2])

    P.dma(nc.sync, idnb[:], idnb_d.ap(), "cs")
    P.dma(nc.sync, idnf[:], idnf_d.ap(), "cs")
    P.dma(nc.sync, h128[:], h128_d.ap(), "cs")
    P.dma(nc.sync, bd64[:], bd64_d.ap(), "cs")
    P.dma(nc.sync, ones1[:], ones_d.ap(), "cs")
    P.dma(nc.sync, maskb_t[:], maskb_d.ap(), "cs")
    P.dma(nc.sync, cst[:], cst_d.ap(), "cs")

    P.wdeps(nc.vector, [P.mark("cs")])
    nc.vector.memset(acc[:], 0.0)
    P.tick(nc.vector.drain(), "ve")
    P.barrier()

    for i in range(6):
        P.dma(nc.gpsimd, wqkv_bf[i * 1024:(i + 1) * 1024, :],
              w_qkv[i * 1024:(i + 1) * 1024, :], "wc")
    for i in range(2):
        P.dma(nc.gpsimd, wout_bf[i * 1024:(i + 1) * 1024, :],
              w_out[i * 1024:(i + 1) * 1024, :], "wc")
    castdone = P.mark("wc")

    from contextlib import ExitStack
    wst = ExitStack()
    w1b = wst.enter_context(nc.sbuf_tensor("w1b", [128, 2 * 2048], F32))
    w1t = wst.enter_context(nc.sbuf_tensor("w1t", [128, 1], F32))
    w2t = wst.enter_context(nc.sbuf_tensor("w2t", [128, 2048], F32))
    w2u = wst.enter_context(nc.sbuf_tensor("w2u", [128, 2048], F32))
    w2o = wst.enter_context(nc.sbuf_tensor("w2o", [128, 2 * 2048], BF16))

    wtiles = []
    for r in range(II // 128):
        wtiles.append((0, w_gate[r * 128:(r + 1) * 128, :]))
    for r in range(II // 128):
        wtiles.append((1, w_up[r * 128:(r + 1) * 128, :]))
    for r in range(E // 128):
        for cb in range(4):
            wtiles.append((2, w_down[r * 128:(r + 1) * 128,
                                     cb * 2048:(cb + 1) * 2048]))
    ws = {"free": [(), ()], "i": 0, "rddone": (), "findone": (),
          "tfree": (), "ofree": [(), ()]}

    def wjob_mean():
        i = ws["i"]; ws["i"] += 1
        wi, srcw = wtiles[i]
        b = i % 2
        buf = w1b[:, b * 2048:(b + 1) * 2048]
        ld = P.dma(nc.scalar, buf, srcw, "wa" if b == 0 else "wb",
                   deps=[ws["free"][b]])
        P.wdeps(nc.vector, [ld])
        nc.vector.reduce_sum(w1t[:], buf, axis=AX.X, apply_absolute_value=True)
        ws["free"][b] = P.tick(nc.vector.drain(), "ve")
        nc.vector.tensor_add(acc[:, wi:wi + 1], acc[:, wi:wi + 1], w1t[:])
        nc.vector.drain()

    def wjob_finalize():
        psw1 = ws["psw1"]
        rddone = P.tick(nc.vector.engine_nop(), "ve")
        for wi in range(3):
            P.wdeps(nc.tensor, [rddone])
            tr = P.tick(nc.tensor.transpose(psw1[0:1, wi * 128:(wi + 1) * 128],
                                            acc[:, wi:wi + 1], idnf[:]), "te")
            P.wdeps(nc.vector, [tr])
            nc.vector.reduce_sum(w1t[0:1, 0:1], psw1[0:1, wi * 128:(wi + 1) * 128],
                                 axis=AX.X)
            nc.vector.drain()
            nc.vector.tensor_scalar(w1t[0:1, 0:1], w1t[0:1, 0:1],
                                    1.0 / float(II * E), EPS, AL.mult, AL.max)
            sc = P.tick(nc.vector.drain(), "ve")
            P.wdeps(nc.tensor, [sc])
            mm = P.tick(nc.tensor.matmul(psw1[0:128, 384 + wi:385 + wi], ones1[:],
                                         w1t[0:1, 0:1], start=True, stop=True), "te")
            P.wdeps(nc.vector, [mm])
            nc.vector.tensor_copy(mbar[:, wi:wi + 1], psw1[:, 384 + wi:385 + wi])
            nc.vector.drain()
            nc.vector.reciprocal(swi[:, wi:wi + 1], mbar[:, wi:wi + 1])
            rddone = P.tick(nc.vector.drain(), "ve")
        ws["findone"] = rddone
        ws["i"] = 0

    def wjob_tern():
        i = ws["i"]; ws["i"] += 1
        wi, srcw = wtiles[i]
        b = i % 2
        buf = w1b[:, b * 2048:(b + 1) * 2048]
        ld = P.dma(nc.scalar, buf, srcw, "wa" if b == 0 else "wb",
                   deps=[ws["free"][b]])
        P.wdeps(nc.vector, [ld, ws["findone"], ws["tfree"]])
        nc.vector.tensor_scalar(w2t[:], buf, swi[:, wi:wi + 1], 0.5,
                                AL.mult, AL.is_gt)
        nc.vector.drain()
        nc.vector.tensor_scalar(w2u[:], buf, swi[:, wi:wi + 1], -0.5,
                                AL.mult, AL.is_lt)
        ws["free"][b] = P.tick(nc.vector.drain(), "ve")
        P.wdeps(nc.vector, [ws["ofree"][b]])
        wi2, r, cb = wdst[i]
        ob = w2o[:, b * 2048:(b + 1) * 2048]
        if wi2 == 0:
            dst = tern_g[r * 128:(r + 1) * 128, :]
            nc.vector.tensor_tensor(ob, w2t[:], w2u[:], AL.subtract)
        elif wi2 == 1:
            dst = tern_u[r * 128:(r + 1) * 128, :]
            nc.vector.tensor_tensor(ob, w2t[:], w2u[:], AL.subtract)
        else:
            obp = bass.AP(w2o, b * 2048, [[2 * 2048, 128], [32, 64], [16, 2], [1, 16]])
            t1p = bass.AP(w2t, 0, [[2048, 128], [1, 64], [64, 2], [128, 16]])
            t2p = bass.AP(w2u, 0, [[2048, 128], [1, 64], [64, 2], [128, 16]])
            nc.vector.tensor_tensor(obp, t1p, t2p, AL.subtract)
            ws["tfree"] = td = P.tick(nc.vector.drain(), "ve")
            sem2 = "sa" if b == 0 else "sb"
            for dlt in range(2):
                dst = bass.AP(wds.tensor, r * 128 * II + cb * 16 + dlt * 64,
                              [[II, 128], [128, 64], [1, 16]])
                obh = bass.AP(w2o, b * 2048 + dlt * 16,
                              [[2 * 2048, 128], [32, 64], [1, 16]])
                P.dma(nc.gpsimd, dst, obh, sem2, deps=[td] if dlt == 0 else ())
            ws["ofree"][b] = P.mark(sem2)
            return
        ws["tfree"] = td = P.tick(nc.vector.drain(), "ve")
        ws["ofree"][b] = P.dma(nc.gpsimd, dst, ob, "sa" if b == 0 else "sb",
                               deps=[td])

    NW = len(wtiles)
    wdst = {}
    for k, (wi, _) in enumerate(wtiles):
        if wi == 0:
            wdst[k] = (0, k, 0)
        elif wi == 1:
            wdst[k] = (1, k - II // 128, 0)
        else:
            kk = k - 2 * (II // 128)
            wdst[k] = (2, kk // 4, kk % 4)

    wjobs = ([wjob_mean] * NW) + [wjob_finalize] + [wjob_tern] * NW

    def wpump(n=1):
        for _ in range(n):
            if wjobs:
                wjobs.pop(0)()

    NT = TH // 128
    with (
        nc.sbuf_tensor("xb1", [128, 2 * 2048], F32) as xb1,
        nc.sbuf_tensor("xnf", [128, 2048], F32) as xnf,
        nc.sbuf_tensor("qt1", [128, 2048], F32) as qt1,
        nc.sbuf_tensor("qi1", [128, 2048], mybir.dt.int32) as qi1,
        nc.sbuf_tensor("qd1", [128, 2048], F32) as qd1,
        nc.sbuf_tensor("qc1", [128, 2048], BF16) as qc1,
        nc.sbuf_tensor("qc2", [128, 2048], BF16) as qc2,
        nc.sbuf_tensor("xnbb", [128, 2 * 2048], BF16) as xnbb,
        nc.sbuf_tensor("xqb", [128, 2 * 2048], BF16) as xqb,
        nc.sbuf_tensor("ln", [128, 40], F32) as ln,
    ):
        xfree = [(), ()]
        bfree = [(), ()]
        qfree = [(), ()]
        prev_rs = ()
        for k in range(NT):
            b = k % 2
            xt = xb1[:, b * 2048:(b + 1) * 2048]
            ld = P.dma(nc.scalar, xt, xc[k * 128:(k + 1) * 128, :],
                       "la" if b == 0 else "lb", deps=[xfree[b]])
            P.wdeps(nc.vector, [ld])
            if prev_rs:
                P.wdeps(nc.vector, [prev_rs])
            for q in range(4):
                nc.vector.bn_stats(ln[:, 16 + 6 * q:16 + 6 * (q + 1)],
                                   xt[:, q * 512:(q + 1) * 512])
                nc.vector.drain()
            nc.vector.bn_aggr(ln[:, 6:8], ln[:, 16:40])
            st = P.tick(nc.vector.drain(), "ve")
            P.wdeps(nc.scalar, [st])
            nc.scalar.drain()
            nc.scalar.activation(ln[:, 8:9], ln[:, 7:8], AF.Ln, bias=EPS, scale=1.0)
            nc.scalar.drain()
            rs = prev_rs = P.tick(nc.scalar.activation(ln[:, 9:10], ln[:, 8:9], AF.Exp,
                                                       bias=0.0, scale=-0.5), "ae")
            P.wdeps(nc.vector, [rs, bfree[b]] if bfree[b] else [rs])
            nc.vector.tensor_scalar(xnbb[:, b * 2048:(b + 1) * 2048], xt,
                                    ln[:, 6:7], ln[:, 9:10], AL.subtract, AL.mult)
            bdone = P.tick(nc.vector.drain(), "ve")
            nc.vector.tensor_scalar(xnf[:], xt, ln[:, 6:7], ln[:, 9:10],
                                    AL.subtract, AL.mult)
            xfree[b] = P.tick(nc.vector.drain(), "ve")
            P.dma(nc.gpsimd, xnb_s[k * 128:(k + 1) * 128, :],
                  xnbb[:, b * 2048:(b + 1) * 2048],
                  "sa" if b == 0 else "sb", deps=[bdone])
            bfree[b] = P.mark("sa" if b == 0 else "sb")
            if k >= HALO // 128:
                nc.vector.reduce_max(ln[:, 10:11], xnf[:], axis=AX.X,
                                     apply_absolute_value=True)
                nc.vector.drain()
                nc.vector.tensor_scalar(ln[:, 11:12], ln[:, 10:11], EPS, 1.0 / 128.0,
                                        AL.max, AL.mult)
                nc.vector.drain()
                nc.vector.tensor_scalar(ln[:, 12:13], ln[:, 10:11], EPS, None, AL.max)
                nc.vector.drain()
                nc.vector.reciprocal(ln[:, 13:14], ln[:, 12:13])
                nc.vector.drain()
                nc.vector.tensor_scalar(ln[:, 14:15], ln[:, 13:14], 128.0, None,
                                        AL.mult)
                nc.vector.drain()
                nc.vector.tensor_scalar(qt1[:], xnf[:], ln[:, 14:15], None, AL.mult)
                nc.vector.drain()
                nc.vector.tensor_scalar(qi1[:], qt1[:], 1024.0, None, AL.add)
                nc.vector.drain()
                nc.vector.scalar_tensor_tensor(qd1[:], qt1[:], 1024.0, qi1[:],
                                               AL.add, AL.subtract)
                nc.vector.drain()
                nc.vector.tensor_scalar(qc1[:], qd1[:], 0.5, None, AL.is_gt)
                nc.vector.drain()
                nc.vector.tensor_scalar(qc2[:], qd1[:], 1.5, None, AL.is_gt)
                nc.vector.drain()
                nc.vector.tensor_scalar(qd1[:], qd1[:], -0.5, None, AL.is_lt)
                nc.vector.drain()
                nc.vector.tensor_sub(qc2[:], qc2[:], qd1[:])
                nc.vector.drain()
                nc.vector.scalar_tensor_tensor(qt1[:], qi1[:], -1024.0, qc1[:],
                                               AL.add, AL.add)
                nc.vector.drain()
                nc.vector.tensor_add(qt1[:], qt1[:], qc2[:])
                nc.vector.drain()
                nc.vector.tensor_scalar(qt1[:], qt1[:], -128.0, 127.0, AL.max, AL.min)
                nc.vector.drain()
                P.wdeps(nc.vector, [qfree[b]])
                nc.vector.tensor_scalar(xqb[:, b * 2048:(b + 1) * 2048], qt1[:],
                                        ln[:, 11:12], None, AL.mult)
                qdone = P.tick(nc.vector.drain(), "ve")
                kq = k - HALO // 128
                P.dma(nc.gpsimd, xnq_s[kq * 128:(kq + 1) * 128, :],
                      xqb[:, b * 2048:(b + 1) * 2048],
                      "sa" if b == 0 else "sb", deps=[qdone])
                qfree[b] = bfree[b] = P.mark("sa" if b == 0 else "sb")
            wpump(3)
    P.barrier()

    sweeps = []
    t0 = 0
    while t0 < TH:
        ts = min(1024, TH - t0)
        sweeps.append((t0, ts))
        t0 += ts
    with (
        nc.sbuf_tensor("xbs", [128, NE * 1024], BF16) as xbs,
        nc.sbuf_tensor("wsl", [128, NE * 1024], BF16) as wsl,
        nc.sbuf_tensor("pje", [128, 2 * 1024], BF16) as pje,
        nc.psum_tensor("psp", [128, 4 * 512], F32) as psp,
        nc.psum_tensor("psw1", [128, 512], F32) as psw1,
    ):
        ws["psw1"] = psw1
        xbs_free = ()
        wsl_free = ()
        pje_free = [(), ()]
        ps_free = [(), ()]
        for (st, ts) in sweeps:
            nsl = ts // 512
            for f in range(NE):
                P.dmat(nc.sync, xbs[:, f * 1024:f * 1024 + ts],
                       xnb_s[st:st + ts, f * 128:(f + 1) * 128], "xs",
                       deps=(xbs_free,) if f == 0 else ())
            xload = P.mark("xs")
            for vh in range(2):
                for f in range(NE):
                    P.dmat(nc.sync, wsl[:, f * 1024:(f + 1) * 1024],
                           wqkv_bf[4096 + vh * 1024:4096 + (vh + 1) * 1024,
                                   f * 128:(f + 1) * 128], "ws",
                           deps=(wsl_free, castdone) if f == 0 else ())
                wload = P.mark("ws")
                P.wdeps(nc.tensor, [xload, wload])
                for tb in range(ts // 128):
                    pb = 2 * (tb % 2)
                    P.wdeps(nc.tensor, [ps_free[tb % 2]] if ps_free[tb % 2] else [])
                    mi = None
                    for f in range(NE):
                        for es in range(2):
                            mi = nc.tensor.matmul(
                                psp[:, (pb + es) * 512:(pb + es + 1) * 512],
                                xbs[:, f * 1024 + tb * 128:f * 1024 + (tb + 1) * 128],
                                wsl[:, f * 1024 + es * 512:f * 1024 + (es + 1) * 512],
                                start=(f == 0), stop=(f == NE - 1))
                    mm = P.tick(mi, "te")
                    P.wdeps(nc.scalar, [mm, pje_free[tb % 2]] if pje_free[tb % 2] else [mm])
                    nc.scalar.drain()
                    ev = P.tick(nc.scalar.activation(
                        pje[:, (tb % 2) * 1024:(tb % 2 + 1) * 1024],
                        psp[:, pb * 512:(pb + 2) * 512], AF.Copy), "ae")
                    ps_free[tb % 2] = ev
                    pje_free[tb % 2] = P.dma(
                        nc.sync, vt_s[st + tb * 128:st + (tb + 1) * 128,
                                      vh * 1024:(vh + 1) * 1024],
                        pje[:, (tb % 2) * 1024:(tb % 2 + 1) * 1024],
                        "so" if tb % 2 == 0 else "sp", deps=[ev])
                    wpump(3)
                wsl_free = P.mark("te")
            for og in range(4):
                for f in range(NE):
                    P.dmat(nc.sync, wsl[:, f * 1024:(f + 1) * 1024],
                           wqkv_bf[og * 1024:(og + 1) * 1024, f * 128:(f + 1) * 128],
                           "ws", deps=(wsl_free, castdone) if f == 0 else ())
                wload = P.mark("ws")
                P.wdeps(nc.tensor, [wload])
                for ot in range(8):
                    pb = nsl * (ot % 2)
                    P.wdeps(nc.tensor, [ps_free[ot % 2]] if ps_free[ot % 2] else [])
                    mi = None
                    for f in range(NE):
                        for sl in range(nsl):
                            mi = nc.tensor.matmul(
                                psp[:, (pb + sl) * 512:(pb + sl + 1) * 512],
                                wsl[:, f * 1024 + ot * 128:f * 1024 + (ot + 1) * 128],
                                xbs[:, f * 1024 + sl * 512:f * 1024 + (sl + 1) * 512],
                                start=(f == 0), stop=(f == NE - 1))
                    mm = P.tick(mi, "te")
                    P.wdeps(nc.scalar, [mm, pje_free[ot % 2]] if pje_free[ot % 2] else [mm])
                    nc.scalar.drain()
                    ev = P.tick(nc.scalar.activation(
                        pje[:, (ot % 2) * 1024:(ot % 2) * 1024 + ts],
                        psp[:, pb * 512:pb * 512 + ts], AF.Copy), "ae")
                    ps_free[ot % 2] = ev
                    orow = og * 1024 + ot * 128
                    dst = q_s if orow < 2048 else k_s
                    orow2 = orow % 2048
                    pje_free[ot % 2] = P.dma(
                        nc.sync, dst[orow2:orow2 + 128, st:st + ts],
                        pje[:, (ot % 2) * 1024:(ot % 2) * 1024 + ts],
                        "so" if ot % 2 == 0 else "sp", deps=[ev])
                    wpump(2)
                wsl_free = P.mark("te")
            xbs_free = P.mark("te")
    P.barrier()

    with (
        nc.sbuf_tensor("Kw", [128, NE * 1024], BF16) as Kw,
        nc.sbuf_tensor("VTw", [128, 8 * 2048], BF16) as VTw,
        nc.sbuf_tensor("Qc", [128, NE * 512], BF16) as Qc,
        nc.sbuf_tensor("Pe", [128, 4 * 640], BF16) as Pe,
        nc.sbuf_tensor("PT", [128, 8 * 512], BF16) as PT,
        nc.sbuf_tensor("Dt", [128, 256], BF16) as Dt,
        nc.sbuf_tensor("Ps", [128, 2 * 640], F32) as Ps,
        nc.sbuf_tensor("maskz_t", [128, 4 * 640], F32) as maskz_t,
        nc.sbuf_tensor("maskg_t", [128, 4 * 640], F32) as maskg_t,
        nc.sbuf_tensor("aveb", [128, 2 * 512], BF16) as aveb,
        nc.sbuf_tensor("sm", [128, 8], F32) as sm,
        nc.psum_tensor("psS", [128, 2 * 1024], F32) as psS,
        nc.psum_tensor("psPT", [128, 2 * 512], F32) as psPT,
        nc.psum_tensor("psAV", [128, 2 * 512], F32) as psAV,
    ):
        ldfree = ()
        s_free = [(), ()]
        pt_free = [(), ()]
        av_free = [(), ()]
        ab_free = [(), ()]
        pe_free = ()
        dt_free = [(), ()]
        ps_free = [(), ()]
        for s in range(4):
            P.dma(nc.sync, maskz_t[:, s * 640:(s + 1) * 640], maskz[s], "cs")
        P.dma(nc.sync, maskg_t[:], maskg_d.ap(), "cs")
        mld = P.mark("cs")
        P.wdeps(nc.vector, [mld])
        for n in range(NCH):
            kw = n * 512
            qg = HALO + n * 512
            P.wdeps(nc.scalar, [ldfree])
            for hb in range(NE):
                P.dma(nc.scalar, Kw[:, hb * 1024:(hb + 1) * 1024],
                      k_s[hb * 128:(hb + 1) * 128, kw:kw + 1024], "ka")
            for kb in range(8):
                P.dma(nc.scalar, VTw[:, kb * 2048:(kb + 1) * 2048],
                      vt_s[kw + kb * 128:kw + (kb + 1) * 128, :], "ka")
            for hb in range(NE):
                P.dma(nc.scalar, Qc[:, hb * 512:(hb + 1) * 512],
                      q_s[hb * 128:(hb + 1) * 128, qg:qg + 512], "ka")
            ld = P.mark("ka")
            P.wdeps(nc.tensor, [ld])
            P.wdeps(nc.vector, [ld])

            def p2b_scores(j):
                h, s = j // 4, j % 4
                pp = j % 2
                ps = psS[:, pp * 1024:pp * 1024 + 640]
                P.wdeps(nc.tensor, [s_free[pp]])
                nc.tensor.matmul(ps[:, 0:512],
                                 Qc[:, h * 512 + s * 128:h * 512 + (s + 1) * 128],
                                 Kw[:, h * 1024 + s * 128:h * 1024 + s * 128 + 512],
                                 start=True, stop=True)
                return P.tick(nc.tensor.matmul(
                    ps[:, 512:640],
                    Qc[:, h * 512 + s * 128:h * 512 + (s + 1) * 128],
                    Kw[:, h * 1024 + s * 128 + 512:h * 1024 + s * 128 + 640],
                    start=True, stop=True), "te")

            def p2b_softmax(j, sc2):
                h, s = j // 4, j % 4
                pp = j % 2
                ps = psS[:, pp * 1024:pp * 1024 + 640]
                msk = maskz_t[:, s * 640:(s + 1) * 640] if n == 0 \
                    else maskg_t[:, s * 640:(s + 1) * 640]
                P.wdeps(nc.vector, [sc2, ps_free[pp]])
                nc.vector.scalar_tensor_tensor(
                    Ps[:, pp * 640:(pp + 1) * 640], ps, SM_SCALE, msk,
                    AL.mult, AL.add)
                nc.vector.drain()
                nc.vector.reduce_max(sm[:, pp * 4:pp * 4 + 1],
                                     Ps[:, pp * 640:(pp + 1) * 640], axis=AX.X)
                nc.vector.drain()
                nc.vector.tensor_scalar(sm[:, pp * 4 + 1:pp * 4 + 2],
                                        sm[:, pp * 4:pp * 4 + 1], -1.0, None,
                                        AL.mult)
                mx = P.tick(nc.vector.drain(), "ve")
                s_free[pp] = mx
                P.wdeps(nc.scalar, [mx, pe_free])
                nc.scalar.drain()
                ps_free[pp] = ex = P.tick(nc.scalar.activation(
                    Pe[:, s * 640:(s + 1) * 640], Ps[:, pp * 640:(pp + 1) * 640],
                    AF.Exp, bias=sm[:, pp * 4 + 1:pp * 4 + 2], scale=1.0,
                    accum_out=sm[:, pp * 4 + 2:pp * 4 + 3]), "ae")
                P.wdeps(nc.vector, [ex, dt_free[pp]])
                nc.vector.reciprocal(sm[:, pp * 4 + 3:pp * 4 + 4],
                                     sm[:, pp * 4 + 2:pp * 4 + 3])
                nc.vector.drain()
                nc.vector.tensor_scalar(Dt[:, pp * 128:(pp + 1) * 128], idnb[:],
                                        sm[:, pp * 4 + 3:pp * 4 + 4], None, AL.mult)
                return P.tick(nc.vector.drain(), "ve")

            def p2b_transposes(j, dv):
                h, s = j // 4, j % 4
                pp = j % 2
                P.wdeps(nc.tensor, [dv])
                for jb in range(5):
                    c = s + jb
                    P.wdeps(nc.tensor, [pt_free[jb % 2]])
                    tr = P.tick(nc.tensor.matmul(
                        psPT[:, (jb % 2) * 512:(jb % 2) * 512 + 128],
                        Pe[:, s * 640 + jb * 128:s * 640 + (jb + 1) * 128],
                        Dt[:, pp * 128:(pp + 1) * 128], start=True, stop=True), "te")
                    if jb == 4:
                        dt_free[pp] = tr
                    P.wdeps(nc.vector, [tr])
                    nc.vector.tensor_copy(
                        PT[:, c * 512 + s * 128:c * 512 + (s + 1) * 128],
                        psPT[:, (jb % 2) * 512:(jb % 2) * 512 + 128])
                    pt_free[jb % 2] = P.tick(nc.vector.drain(), "ve")

            def p2b_av(h):
                nonlocal pe_free
                hp = h % 2
                pa = psAV[:, hp * 512:(hp + 1) * 512]
                P.wdeps(nc.tensor, [av_free[hp], P.mark("ve")])
                mi = None
                corder = [4] + [c for c in range(8) if c != 4]
                for ci_, c in enumerate(corder):
                    a0 = max(0, c - 4) * 128
                    a1 = (min(3, c) + 1) * 128
                    mi = nc.tensor.matmul(pa[:, a0:a1],
                                          VTw[:, c * 2048 + h * 128:c * 2048 + (h + 1) * 128],
                                          PT[:, c * 512 + a0:c * 512 + a1],
                                          start=(ci_ == 0), stop=(ci_ == 7))
                pe_free = av = P.tick(mi, "te")
                P.wdeps(nc.scalar, [av, ab_free[hp]])
                nc.scalar.drain()
                av_free[hp] = ev = P.tick(nc.scalar.activation(
                    aveb[:, hp * 512:(hp + 1) * 512], pa, AF.Copy), "ae")
                ab_free[hp] = P.dma(nc.sync,
                                    att_s[h * 128:(h + 1) * 128, n * 512:(n + 1) * 512],
                                    aveb[:, hp * 512:(hp + 1) * 512],
                                    "so" if hp == 0 else "sp", deps=[ev])

            prev = None
            for j in range(NH * 4 + 1):
                if j < NH * 4:
                    sc2 = p2b_scores(j)
                if prev is not None:
                    jp, sc2p = prev
                    dv = p2b_softmax(jp, sc2p)
                    p2b_transposes(jp, dv)
                    if jp % 4 == 3:
                        p2b_av(jp // 4)
                prev = (j, sc2) if j < NH * 4 else None
            ldfree = P.mark("te")
        while wjobs:
            wpump(1)
    wst.close()
    P.barrier()

    with (
        nc.sbuf_tensor("xqs", [128, NE * 2048], BF16) as xqs,
        nc.sbuf_tensor("tsl", [128, 2 * NE * 1024], BF16) as tsl,
        nc.sbuf_tensor("sg", [128, 8 * 2048], BF16) as sg,
        nc.sbuf_tensor("ie", [128, 2 * 2048], BF16) as ie,
        nc.psum_tensor("psf", [128, 8 * 512], F32) as psf,
    ):
        TSL = max(1, T // 512)
        for f in range(NE):
            P.dmat(nc.sync, xqs[:, f * 2048:f * 2048 + T],
                   xnq_s[0:T, f * 128:(f + 1) * 128], "xs")
        xqload = P.mark("xs")
        passes = []
        for og in range(II // 1024):
            passes.append(("g", og))
            passes.append(("u", og))
        slab_ready = {}
        slab_free = [(), ()]

        def load_slab(i):
            w, og = passes[i]
            b = i % 2
            src = tern_g if w == "g" else tern_u
            sname = "ws" if b == 0 else "wt"
            for f in range(NE):
                P.dmat(nc.sync, tsl[:, b * NE * 1024 + f * 1024:b * NE * 1024 + (f + 1) * 1024],
                       src[og * 1024:(og + 1) * 1024, f * 128:(f + 1) * 128], sname,
                       deps=(slab_free[b],) if f == 0 else ())
            slab_ready[i] = P.mark(sname)

        load_slab(0)
        load_slab(1)
        ps_free = [(), ()]
        sg_done = [()] * 8
        sg_free = [()] * 8
        ie_free = [(), ()]
        for i, (w, og) in enumerate(passes):
            b = i % 2
            P.wdeps(nc.tensor, [slab_ready[i], xqload])
            for ot in range(8):
                pb = (ot % 2) * 4
                P.wdeps(nc.tensor, [ps_free[ot % 2]] if ps_free[ot % 2] else [])
                mi = None
                for f in range(NE):
                    for sl in range(TSL):
                        mi = nc.tensor.matmul(
                            psf[:, (pb + sl) * 512:(pb + sl + 1) * 512],
                            tsl[:, b * NE * 1024 + f * 1024 + ot * 128:b * NE * 1024 + f * 1024 + (ot + 1) * 128],
                            xqs[:, f * 2048 + sl * 512:f * 2048 + (sl + 1) * 512],
                            start=(f == 0), stop=(f == NE - 1))
                mm = P.tick(mi, "te")
                if w == "g":
                    P.wdeps(nc.scalar, [mm])
                    if sg_free[ot]:
                        P.wdeps(nc.scalar, [sg_free[ot]])
                    nc.scalar.drain()
                    ps_free[ot % 2] = sg_done[ot] = P.tick(
                        nc.scalar.activation(sg[:, ot * 2048:ot * 2048 + T],
                                             psf[:, pb * 512:pb * 512 + T],
                                             AF.Silu, bias=0.0,
                                             scale=mbar[:, 0:1]), "ae")
                else:
                    P.wdeps(nc.vector, [mm, sg_done[ot]])
                    if ie_free[ot % 2]:
                        P.wdeps(nc.vector, [ie_free[ot % 2]])
                    nc.vector.scalar_tensor_tensor(
                        ie[:, (ot % 2) * 2048:(ot % 2) * 2048 + T],
                        psf[:, pb * 512:pb * 512 + T],
                        mbar[:, 1:2], sg[:, ot * 2048:ot * 2048 + T],
                        AL.mult, AL.mult)
                    ps_free[ot % 2] = iv = sg_free[ot] = P.tick(nc.vector.drain(), "ve")
                    ie_free[ot % 2] = P.dma(
                        nc.gpsimd,
                        inter_s[og * 1024 + ot * 128:og * 1024 + (ot + 1) * 128, 0:T],
                        ie[:, (ot % 2) * 2048:(ot % 2) * 2048 + T],
                        "sa" if ot % 2 == 0 else "sb", deps=[iv])
            slab_free[b] = P.mark("te")
            if i + 2 < len(passes):
                load_slab(i + 2)
    P.barrier()

    NTILE = max(1, T // 1024)
    TT = T // NTILE
    NTG = TT // 128
    with (
        nc.sbuf_tensor("bufA", [128, NI * TT], BF16) as bufA,
        nc.sbuf_tensor("zz", [128, 2 * 1024], BF16) as zz,
        nc.sbuf_tensor("Mx", [128, TT], F32) as Mx,
        nc.sbuf_tensor("syc", [128, 4 * 16], F32) as syc,
        nc.sbuf_tensor("syr", [1, 2 * TT], F32) as syr,
        nc.sbuf_tensor("Sb", [128, 2 * TT], F32) as Sb,
        nc.sbuf_tensor("qtm", [128, TT], F32) as qtm,
        nc.sbuf_tensor("qi2", [128, TT], mybir.dt.int32) as qi2,
        nc.sbuf_tensor("qd2", [128, TT], F32) as qd2,
        nc.sbuf_tensor("qc3", [128, TT], BF16) as qc3,
        nc.sbuf_tensor("qc4", [128, TT], BF16) as qc4,
        nc.sbuf_tensor("wdsl", [128, 3 * 1024], BF16) as wdsl,
        nc.sbuf_tensor("atl", [128, 8 * 512], BF16) as atl,
        nc.sbuf_tensor("xr", [128, 2 * 512], F32) as xr,
        nc.sbuf_tensor("fo", [128, 2 * 512], F32) as fo,
    ):
        buf_free = ()
        atl_free = ()
        wfree = [(), (), ()]
        x_free = [(), ()]
        f_free = [(), ()]
        for tt in range(NTILE):
            tof = tt * TT
            P.wdeps(nc.scalar, [buf_free])
            for ci in range(NI):
                P.dma(nc.scalar, bufA[:, ci * TT:(ci + 1) * TT],
                      inter_s[ci * 128:(ci + 1) * 128, tof:tof + TT], "ka")
            ild = P.mark("ka")
            P.wdeps(nc.tensor, [ild])
            with nc.psum_tensor(f"psZ{tt}", [128, 2 * 512], F32) as psZ:
                z_free = [(), ()]
                for ci in range(NI):
                    for th in range(TT // 512):
                        j = ci * (TT // 512) + th
                        pz = j % 2
                        P.wdeps(nc.tensor, [z_free[pz]] if z_free[pz] else [])
                        mm = P.tick(nc.tensor.matmul(
                            psZ[:, pz * 512:(pz + 1) * 512], h128[:],
                            bufA[:, ci * TT + th * 512:ci * TT + (th + 1) * 512],
                            start=True, stop=True), "te")
                        P.wdeps(nc.scalar, [mm])
                        nc.scalar.drain()
                        z_free[pz] = P.tick(nc.scalar.activation(
                            bufA[:, ci * TT + th * 512:ci * TT + (th + 1) * 512],
                            psZ[:, pz * 512:(pz + 1) * 512], AF.Copy), "ae")
                s1done = P.mark("ae")
            with (
                nc.psum_tensor(f"psT{tt}", [128, 2 * 1024], BF16) as psT,
                nc.psum_tensor(f"psY{tt}", [128, 2 * 1024], F32) as psY,
            ):
                P.wdeps(nc.tensor, [s1done])
                t_free = [(), ()]
                y_free = [(), ()]
                zz_free = [(), ()]
                for g in range(TT // 16):
                    gp = g % 2
                    P.wdeps(nc.tensor, [t_free[gp]])
                    mi = None
                    for tl in range(16):
                        t = g * 16 + tl
                        for d in range(2):
                            mi = nc.tensor.transpose(
                                psT[d * 64:(d + 1) * 64,
                                    gp * 1024 + tl * 64:gp * 1024 + (tl + 1) * 64],
                                bufA[d * 64:(d + 1) * 64, t::TT],
                                idnb[d * 64:(d + 1) * 64, d * 64:(d + 1) * 64])
                    tr = P.tick(mi, "te")
                    P.wdeps(nc.vector, [tr, zz_free[gp]])
                    nc.vector.tensor_copy(zz[:, gp * 1024:(gp + 1) * 1024],
                                          psT[:, gp * 1024:(gp + 1) * 1024])
                    t_free[gp] = zc = P.tick(nc.vector.drain(), "ve")
                    P.wdeps(nc.tensor, [zc, y_free[gp]])
                    nc.tensor.matmul(psY[:, gp * 1024:gp * 1024 + 512], bd64[:],
                                     zz[:, gp * 1024:gp * 1024 + 512],
                                     start=True, stop=True)
                    s2 = P.tick(nc.tensor.matmul(psY[:, gp * 1024 + 512:(gp + 1) * 1024],
                                                 bd64[:],
                                                 zz[:, gp * 1024 + 512:(gp + 1) * 1024],
                                                 start=True, stop=True), "te")
                    zz_free[gp] = s2
                    P.wdeps(nc.vector, [s2])
                    nc.vector.reduce_max(
                        Mx[:, g * 16:(g + 1) * 16],
                        psY[:, gp * 1024:(gp + 1) * 1024].rearrange("p (a b) -> p a b", a=16),
                        axis=AX.X, apply_absolute_value=True)
                    nc.vector.drain()
                    dst = bass.AP(bufA, g * 16, [[NI * TT, 128], [1, 16], [TT, 64]])
                    src2 = psY[:, gp * 1024:(gp + 1) * 1024].rearrange("p (a b) -> p a b", a=16)
                    nc.vector.tensor_copy(dst, src2)
                    y_free[gp] = P.tick(nc.vector.drain(), "ve")
                hdone_ae = P.mark("ve")
                hdone_ve = P.mark("ve")
            with nc.psum_tensor(f"psA{tt}", [128, 2 * 512], F32) as psA:
                P.wdeps(nc.tensor, [hdone_ae, hdone_ve])
                a_free = [(), ()]
                for tg in range(NTG):
                    ap = tg % 2
                    P.wdeps(nc.tensor, [a_free[ap]] if a_free[ap] else [])
                    tr = P.tick(nc.tensor.transpose(
                        psA[:, ap * 512:ap * 512 + 128],
                        Mx[:, tg * 128:(tg + 1) * 128], idnf[:]), "te")
                    P.wdeps(nc.vector, [tr])
                    nc.vector.reduce_max(syc[:, tg:tg + 1],
                                         psA[:, ap * 512:ap * 512 + 128], axis=AX.X)
                    a_free[ap] = P.tick(nc.vector.drain(), "ve")
                for tg in range(NTG):
                    nc.vector.tensor_scalar(syc[:, 16 + tg:17 + tg], syc[:, tg:tg + 1],
                                            NRM, EPS, AL.mult, AL.max)
                    nc.vector.drain()
                    nc.vector.tensor_scalar(syc[:, 32 + tg:33 + tg],
                                            syc[:, 16 + tg:17 + tg], 1.0 / 128.0,
                                            None, AL.mult)
                    nc.vector.drain()
                    nc.vector.reciprocal(syc[:, 48 + tg:49 + tg],
                                         syc[:, 16 + tg:17 + tg])
                    nc.vector.drain()
                scdone = P.tick(nc.vector.engine_nop(), "ve")
                P.wdeps(nc.tensor, [scdone])
                for tg in range(NTG):
                    for half, col in ((0, 48), (1, 32)):
                        ap = tg % 2
                        P.wdeps(nc.tensor, [a_free[ap]] if a_free[ap] else [])
                        tr = P.tick(nc.tensor.transpose(
                            psA[0:1, ap * 512:ap * 512 + 128],
                            syc[:, col + tg:col + tg + 1], idnf[:]), "te")
                        P.wdeps(nc.vector, [tr])
                        nc.vector.tensor_copy(
                            syr[0:1, half * TT + tg * 128:half * TT + (tg + 1) * 128],
                            psA[0:1, ap * 512:ap * 512 + 128])
                        a_free[ap] = P.tick(nc.vector.drain(), "ve")
                rdone = P.mark("ve")
                P.wdeps(nc.tensor, [rdone])
                for half in range(2):
                    for sl in range(TT // 512):
                        ap = sl % 2
                        P.wdeps(nc.tensor, [a_free[ap]] if a_free[ap] else [])
                        bm = P.tick(nc.tensor.matmul(
                            psA[:, ap * 512:(ap + 1) * 512], ones1[:],
                            syr[0:1, half * TT + sl * 512:half * TT + (sl + 1) * 512],
                            start=True, stop=True), "te")
                        P.wdeps(nc.vector, [bm])
                        nc.vector.tensor_copy(
                            Sb[:, half * TT + sl * 512:half * TT + (sl + 1) * 512],
                            psA[:, ap * 512:(ap + 1) * 512])
                        a_free[ap] = P.tick(nc.vector.drain(), "ve")
            if dbg and tt == 0:
                P.wdeps(nc.gpsimd, [hdone_ae, P.mark("ve")])
                for u in range(NI):
                    P.dma(nc.gpsimd, dbg_h[:, u * TT:(u + 1) * TT],
                          bufA[:, u * TT:(u + 1) * TT], "wc")
                P.dma(nc.gpsimd, dbg_m[:, 0:TT], Mx[:, 0:TT], "wc")
                P.dma(nc.gpsimd, dbg_m[:, TT:2 * TT], Sb[:, 0:TT], "wc")
                P.dma(nc.gpsimd, dbg_m[:, 2 * TT:3 * TT], Sb[:, TT:2 * TT], "wc")
                dhd = P.mark("wc")
                P.wdeps(nc.vector, [dhd])
            P.wdeps(nc.vector, [hdone_ae])
            qdone_u = []
            for u in range(NI):
                nc.vector.scalar_tensor_tensor(qtm[:, 0:TT], bufA[:, u * TT:(u + 1) * TT],
                                               128.0 * NRM, Sb[:, 0:TT], AL.mult, AL.mult)
                nc.vector.drain()
                nc.vector.tensor_scalar(qi2[:], qtm[:, 0:TT], 1024.0, None, AL.add)
                nc.vector.drain()
                nc.vector.scalar_tensor_tensor(qd2[:], qtm[:, 0:TT], 1024.0, qi2[:],
                                               AL.add, AL.subtract)
                nc.vector.drain()
                nc.vector.tensor_scalar(qc3[:], qd2[:], 0.5, None, AL.is_gt)
                nc.vector.drain()
                nc.vector.tensor_scalar(qc4[:], qd2[:], 1.5, None, AL.is_gt)
                nc.vector.drain()
                nc.vector.tensor_scalar(qd2[:], qd2[:], -0.5, None, AL.is_lt)
                nc.vector.drain()
                nc.vector.tensor_sub(qc4[:], qc4[:], qd2[:])
                nc.vector.drain()
                nc.vector.scalar_tensor_tensor(qtm[:, 0:TT], qi2[:], -1024.0, qc3[:],
                                               AL.add, AL.add)
                nc.vector.drain()
                nc.vector.tensor_add(qtm[:, 0:TT], qtm[:, 0:TT], qc4[:])
                nc.vector.drain()
                nc.vector.tensor_scalar(qtm[:, 0:TT], qtm[:, 0:TT], -128.0, 127.0,
                                        AL.max, AL.min)
                nc.vector.drain()
                nc.vector.scalar_tensor_tensor(bufA[:, u * TT:(u + 1) * TT], qtm[:, 0:TT],
                                               mbar[:, 2:3], Sb[:, TT:2 * TT],
                                               AL.mult, AL.mult)
                qdone_u.append(P.tick(nc.vector.drain(), "ve"))
            qdone = qdone_u[-1]
            if dbg and tt == 0:
                P.wdeps(nc.gpsimd, [qdone])
                for u in range(NI):
                    P.dma(nc.gpsimd, dbg_yq[:, u * TT:(u + 1) * TT],
                          bufA[:, u * TT:(u + 1) * TT], "wc")
                qdone = P.mark("wc")
                P.wdeps(nc.vector, [qdone])
                qdone = P.tick(nc.vector.engine_nop(), "ve")
            with nc.psum_tensor(f"psD{tt}", [128, 8 * 512], F32) as psD:
                st = {"d_free": (), "atl_free": atl_free}

                def f2b_mms(th, eh, qdone, ald):
                    P.wdeps(nc.scalar, [st["atl_free"]])
                    for ec in range(8):
                        P.dma(nc.scalar, atl[:, ec * 512:(ec + 1) * 512],
                              att_s[ec * 128:(ec + 1) * 128,
                                    tof + th * 512:tof + (th + 1) * 512], "kb")
                    ald = P.mark("kb")
                    ald2 = ()
                    sl_ready = [P.dmat(nc.sync, wdsl[:, 0:1024],
                                       wds[eh * 1024:(eh + 1) * 1024, 0:128], "ws",
                                       deps=[wfree[0]]),
                                P.dmat(nc.sync, wdsl[:, 1024:2048],
                                       wds[eh * 1024:(eh + 1) * 1024, 128:256], "wt",
                                       deps=[wfree[1]])]
                    P.wdeps(nc.tensor, [st["d_free"]])
                    for u in range(NI + NE):
                        bs = u % 3
                        P.wdeps(nc.tensor, [sl_ready[u]])
                        if u < NI:
                            P.wdeps(nc.tensor, [qdone_u[u]])
                        elif u == NI:
                            P.wdeps(nc.tensor, [ald])
                        elif u == NI + 8:
                            P.wdeps(nc.tensor, [ald2])
                        mi = None
                        for tg in range(4):
                            for es in range(2):
                                if u < NI:
                                    lhs = bufA[:, u * TT + th * 512 + tg * 128:
                                               u * TT + th * 512 + (tg + 1) * 128]
                                else:
                                    ecl = (u - NI) % 8
                                    lhs = atl[:, ecl * 512 + tg * 128:
                                              ecl * 512 + (tg + 1) * 128]
                                mi = nc.tensor.matmul(
                                    psD[:, (tg * 2 + es) * 512:(tg * 2 + es + 1) * 512],
                                    lhs,
                                    wdsl[:, bs * 1024 + es * 512:bs * 1024 + (es + 1) * 512],
                                    start=(u == 0), stop=(u == NI + NE - 1))
                        wfree[bs] = P.tick(mi, "te")
                        if u == NI + 7:
                            for ec in range(8, NE):
                                P.dma(nc.scalar, atl[:, (ec - 8) * 512:(ec - 7) * 512],
                                      att_s[ec * 128:(ec + 1) * 128,
                                            tof + th * 512:tof + (th + 1) * 512], "kb",
                                      deps=[wfree[bs]] if ec == 8 else ())
                            ald2 = P.mark("kb")
                        if u + 2 < NI + NE:
                            nb = (u + 2) % 3
                            nname = ("ws", "wt", "wu")[nb]
                            if u + 2 < NI:
                                srcw = wds[eh * 1024:(eh + 1) * 1024,
                                           (u + 2) * 128:(u + 3) * 128]
                            else:
                                ec2 = u + 2 - NI
                                srcw = wout_bf[eh * 1024:(eh + 1) * 1024,
                                               ec2 * 128:(ec2 + 1) * 128]
                            P.dmat(nc.sync, wdsl[:, nb * 1024:(nb + 1) * 1024],
                                   srcw, nname, deps=[wfree[nb]])
                            sl_ready.append(P.mark(nname))
                    return P.mark("te")

                def f2b_evac(th, eh, dmm, tof):
                    for tg in range(4):
                        for es in range(2):
                            bp = es % 2
                            gr = HALO + tof + th * 512 + tg * 128
                            xl = P.dma(nc.scalar, xr[:, bp * 512:(bp + 1) * 512],
                                       xc[gr:gr + 128,
                                          eh * 1024 + es * 512:eh * 1024 + (es + 1) * 512],
                                       "xa" if bp == 0 else "xb",
                                       deps=[x_free[bp]])
                            P.wdeps(nc.vector, [dmm, xl, f_free[bp]])
                            nc.vector.tensor_add(
                                fo[:, bp * 512:(bp + 1) * 512],
                                psD[:, (tg * 2 + es) * 512:(tg * 2 + es + 1) * 512],
                                xr[:, bp * 512:(bp + 1) * 512])
                            x_free[bp] = fv = P.tick(nc.vector.drain(), "ve")
                            f_free[bp] = P.dma(
                                nc.sync,
                                out_c[tof + th * 512 + tg * 128:tof + th * 512 + (tg + 1) * 128,
                                      eh * 1024 + es * 512:eh * 1024 + (es + 1) * 512],
                                fo[:, bp * 512:(bp + 1) * 512],
                                "so" if bp == 0 else "sp", deps=[fv])
                    st["d_free"] = P.mark("ve")

                for th in range(TT // 512):
                    for eh in range(2):
                        dmm = f2b_mms(th, eh, qdone, ())
                        f2b_evac(th, eh, dmm, tof)
                    st["atl_free"] = P.mark("te")
                atl_free = st["atl_free"]
            buf_free = P.mark("te")
    for s, eng in P.dmaeng.items():
        if P.cnt[s] > 0:
            eng.wait_ge(P.sem[s], P.cnt[s])
    P.close()
    return nc


def _host_inputs(x, T):
    B, S, _ = x.shape
    percore = []
    npc = S // T
    for c in range(8):
        b, pos = c // npc, (c % npc) * T
        xcv = np.zeros((T + HALO, E), np.float32)
        lo = pos - HALO
        if lo < 0:
            xcv[HALO:, :] = x[b, pos:pos + T]
        else:
            xcv[:, :] = x[b, lo:pos + T]
        mz = np.zeros((4, 128, 640), np.float32)
        ppv = np.arange(128)[:, None]
        for s in range(4):
            jlv = np.arange(640)[None, :]
            ok = (jlv >= ppv) & (jlv <= ppv + 512)
            if pos == 0:
                ok &= (s * 128 + jlv) >= HALO
            mz[s] = np.where(ok, 0.0, NEGM * SM_SCALE)
        percore.append({"xc": xcv, "maskz": mz})
    return percore


def kernel(L0_attn_x, L0_attn_w_qkv, L0_attn_w_out,
           L0_ffn_w_gate, L0_ffn_w_up, L0_ffn_w_down):
    x = np.ascontiguousarray(np.asarray(L0_attn_x, dtype=np.float32))
    B, S, _ = x.shape
    T = (B * S) // 8
    nc = build_nc(T=T)
    w = {
        "w_qkv": np.ascontiguousarray(L0_attn_w_qkv, dtype=np.float32),
        "w_out": np.ascontiguousarray(L0_attn_w_out, dtype=np.float32),
        "w_gate": np.ascontiguousarray(L0_ffn_w_gate, dtype=np.float32),
        "w_up": np.ascontiguousarray(L0_ffn_w_up, dtype=np.float32),
        "w_down": np.ascontiguousarray(L0_ffn_w_down, dtype=np.float32),
    }
    in_maps = []
    for pc in _host_inputs(x, T):
        m = dict(w)
        m.update(pc)
        in_maps.append(m)
    res = run_bass_kernel_spmd(nc, in_maps, core_ids=list(range(8)))
    out = np.empty((B, S, E), np.float32)
    npc = S // T
    for c in range(8):
        b, pos = c // npc, (c % npc) * T
        out[b, pos:pos + T] = res.results[c]["out_c"]
    return out


# version 13
# speedup vs baseline: 1.0217x; 1.0217x over previous
import numpy as np
import ml_dtypes
import concourse.bass as bass
import concourse.mybir as mybir
from concourse.bass_utils import run_bass_kernel_spmd

F32 = mybir.dt.float32
BF16 = mybir.dt.bfloat16
AL = mybir.AluOpType
AF = mybir.ActivationFunctionType
AX = mybir.AxisListType

E = 2048
NH = 16
DH = 128
O3 = 6144
II = 8192
CH = 512
NE = E // 128
NI = II // 128

MAGIC = 12582912.0
NEGM = -1e30
SM_SCALE = 1.0 / float(np.sqrt(128.0))
NRM = float(np.float32(1.0) / np.float32(np.sqrt(8192.0)))
EPS = 1e-5
HALO = 512


def _had(n):
    i = np.arange(n)
    pc = np.array([bin(v).count("1") for v in range(n)])
    m = pc[i[:, None] & i[None, :]]
    return np.where(m % 2 == 0, 1.0, -1.0).astype(np.float32)


class Prog:

    def __init__(self, nc):
        self.nc = nc
        self.cm = []
        self.sem = {}
        self.cnt = {}
        self.dmaeng = {}

    def enter(self, cm):
        v = cm.__enter__()
        self.cm.append(cm)
        return v

    def close(self):
        for cm in reversed(self.cm):
            cm.__exit__(None, None, None)
        self.cm = []

    def mksem(self, *names):
        for n in names:
            self.sem[n] = self.enter(self.nc.semaphore(n))
            self.cnt[n] = 0

    def mkdma(self, eng, *names):
        for n in names:
            self.mksem(n)
            self.dmaeng[n] = eng

    def dma(self, eng, dst, src, s, deps=()):
        self.wdeps(eng, deps)
        eng.dma_start(dst, src).then_inc(self.sem[s], 16)
        self.cnt[s] += 16
        return (s, self.cnt[s])

    def dmat(self, eng, dst, src, s, deps=()):
        self.wdeps(eng, deps)
        eng.dma_start_transpose(dst, src).then_inc(self.sem[s], 16)
        self.cnt[s] += 16
        return (s, self.cnt[s])

    def tick(self, instr, s, k=1):
        instr.then_inc(self.sem[s], k)
        self.cnt[s] += k
        return (s, self.cnt[s])

    def mark(self, s):
        return (s, self.cnt[s])

    def wdeps(self, eng, deps):
        for d in deps:
            if not d:
                continue
            (s, v) = d
            if v > 0:
                eng.wait_ge(self.sem[s], v)

    def barrier(self):
        for s, eng in self.dmaeng.items():
            if self.cnt[s] > 0:
                eng.wait_ge(self.sem[s], self.cnt[s])
        self.nc.all_engine_barrier()


def build_nc(T=2048, dbg=False):
    TH = T + HALO
    NCH = T // CH
    nc = bass.Bass("TRN2", target_bir_lowering=False)

    xc = nc.dram_tensor("xc", [TH, E], F32, kind="ExternalInput").ap()
    w_qkv = nc.dram_tensor("w_qkv", [O3, E], F32, kind="ExternalInput").ap()
    w_out = nc.dram_tensor("w_out", [E, E], F32, kind="ExternalInput").ap()
    w_gate = nc.dram_tensor("w_gate", [II, E], F32, kind="ExternalInput").ap()
    w_up = nc.dram_tensor("w_up", [II, E], F32, kind="ExternalInput").ap()
    w_down = nc.dram_tensor("w_down", [E, II], F32, kind="ExternalInput").ap()
    maskz = nc.dram_tensor("maskz", [4, 128, 640], F32, kind="ExternalInput").ap()
    out_c = nc.dram_tensor("out_c", [T, E], F32, kind="ExternalOutput").ap()

    wqkv_bf = nc.dram_tensor("wqkv_bf", [O3, E], BF16).ap()
    wout_bf = nc.dram_tensor("wout_bf", [E, E], BF16).ap()
    tern_g = nc.dram_tensor("tern_g", [II, E], BF16).ap()
    tern_u = nc.dram_tensor("tern_u", [II, E], BF16).ap()
    wds = nc.dram_tensor("wds", [E, II], BF16).ap()
    xnb_s = nc.dram_tensor("xnb_s", [TH, E], BF16).ap()
    xnq_s = nc.dram_tensor("xnq_s", [T, E], BF16).ap()
    q_s = nc.dram_tensor("q_s", [E, TH], BF16).ap()
    k_s = nc.dram_tensor("k_s", [E, TH], BF16).ap()
    vt_s = nc.dram_tensor("vt_s", [TH, E], BF16).ap()
    att_s = nc.dram_tensor("att_s", [E, T], BF16).ap()
    inter_s = nc.dram_tensor("inter_s", [II, T], BF16).ap()
    if dbg:
        dbg_yq = nc.dram_tensor("dbg_yq", [128, II * (T // max(1, T // 1024)) // 128], F32).ap()
        dbg_h = nc.dram_tensor("dbg_h", [128, II * (T // max(1, T // 1024)) // 128], F32).ap()
        dbg_m = nc.dram_tensor("dbg_m", [128, 4 * (T // max(1, T // 1024))], F32).ap()

    h128_d = nc.inline_tensor(_had(128).astype(ml_dtypes.bfloat16), "h128c")
    h64 = _had(64)
    bdm = np.zeros((128, 128), np.float32)
    bdm[:64, :64] = h64
    bdm[64:, 64:] = h64
    bd64_d = nc.inline_tensor(bdm.astype(ml_dtypes.bfloat16), "bd64c")
    idnb_d = nc.inline_tensor(np.eye(128, dtype=ml_dtypes.bfloat16), "idnbc")
    idnf_d = nc.inline_tensor(np.eye(128, dtype=np.float32), "idnfc")
    ones_d = nc.inline_tensor(np.ones((1, 128), np.float32), "ones1c")
    pp = np.arange(128)[:, None]
    jl = np.arange(128)[None, :]
    mb = np.zeros((128, 256), np.float32)
    mb[:, 0:128] = np.where(jl >= pp, 0.0, NEGM)
    mb[:, 128:256] = np.where(jl <= pp, 0.0, NEGM)
    maskb_d = nc.inline_tensor(mb, "maskbc")
    cst_d = nc.inline_tensor(
        np.tile(np.array([[MAGIC, EPS]], np.float32), (128, 1)), "cstc")

    P = Prog(nc)
    P.mksem("te", "ve", "ae")
    P.mkdma(nc.sync, "cs", "xs", "ws", "wt", "wu", "so", "sp")
    P.mkdma(nc.scalar, "la", "lb", "ka", "kb", "xa", "xb")
    P.mkdma(nc.gpsimd, "wc", "sa", "sb", "wa", "wb")
    sb = P.enter

    idnb = sb(nc.sbuf_tensor("idnb", [128, 128], BF16))
    idnf = sb(nc.sbuf_tensor("idnf", [128, 128], F32))
    h128 = sb(nc.sbuf_tensor("h128", [128, 128], BF16))
    bd64 = sb(nc.sbuf_tensor("bd64", [128, 128], BF16))
    ones1 = sb(nc.sbuf_tensor("ones1", [1, 128], F32))
    maskb_t = sb(nc.sbuf_tensor("maskb_t", [128, 256], F32))
    maskz_t = sb(nc.sbuf_tensor("maskz_t", [128, 4 * 640], F32))
    acc = sb(nc.sbuf_tensor("acc", [128, 3], F32))
    mbar = sb(nc.sbuf_tensor("mbar", [128, 3], F32))
    swi = sb(nc.sbuf_tensor("swi", [128, 3], F32))
    cst = sb(nc.sbuf_tensor("cst", [128, 2], F32))
    nc.const_aps.register(F32, MAGIC, cst[:, 0:1])
    nc.const_aps.register(F32, EPS, cst[:, 1:2])

    P.dma(nc.sync, idnb[:], idnb_d.ap(), "cs")
    P.dma(nc.sync, idnf[:], idnf_d.ap(), "cs")
    P.dma(nc.sync, h128[:], h128_d.ap(), "cs")
    P.dma(nc.sync, bd64[:], bd64_d.ap(), "cs")
    P.dma(nc.sync, ones1[:], ones_d.ap(), "cs")
    P.dma(nc.sync, maskb_t[:], maskb_d.ap(), "cs")
    P.dma(nc.sync, cst[:], cst_d.ap(), "cs")
    for s in range(4):
        P.dma(nc.sync, maskz_t[:, s * 640:(s + 1) * 640], maskz[s], "cs")
    P.wdeps(nc.vector, [P.mark("cs")])
    nc.vector.memset(acc[:], 0.0)
    P.tick(nc.vector.drain(), "ve")
    P.barrier()

    for i in range(6):
        P.dma(nc.gpsimd, wqkv_bf[i * 1024:(i + 1) * 1024, :],
              w_qkv[i * 1024:(i + 1) * 1024, :], "wc")
    for i in range(2):
        P.dma(nc.gpsimd, wout_bf[i * 1024:(i + 1) * 1024, :],
              w_out[i * 1024:(i + 1) * 1024, :], "wc")
    castdone = P.mark("wc")

    from contextlib import ExitStack
    wst = ExitStack()
    w1b = wst.enter_context(nc.sbuf_tensor("w1b", [128, 2 * 2048], F32))
    w1t = wst.enter_context(nc.sbuf_tensor("w1t", [128, 1], F32))
    w2t = wst.enter_context(nc.sbuf_tensor("w2t", [128, 2048], F32))
    w2u = wst.enter_context(nc.sbuf_tensor("w2u", [128, 2048], F32))
    w2o = wst.enter_context(nc.sbuf_tensor("w2o", [128, 2 * 2048], BF16))

    wtiles = []
    for r in range(II // 128):
        wtiles.append((0, w_gate[r * 128:(r + 1) * 128, :]))
    for r in range(II // 128):
        wtiles.append((1, w_up[r * 128:(r + 1) * 128, :]))
    for r in range(E // 128):
        for cb in range(4):
            wtiles.append((2, w_down[r * 128:(r + 1) * 128,
                                     cb * 2048:(cb + 1) * 2048]))
    ws = {"free": [(), ()], "i": 0, "rddone": (), "findone": (),
          "tfree": (), "ofree": [(), ()]}

    def wjob_mean():
        i = ws["i"]; ws["i"] += 1
        wi, srcw = wtiles[i]
        b = i % 2
        buf = w1b[:, b * 2048:(b + 1) * 2048]
        ld = P.dma(nc.gpsimd, buf, srcw, "wa" if b == 0 else "wb",
                   deps=[ws["free"][b]])
        P.wdeps(nc.vector, [ld])
        nc.vector.reduce_sum(w1t[:], buf, axis=AX.X, apply_absolute_value=True)
        ws["free"][b] = P.tick(nc.vector.drain(), "ve")
        nc.vector.tensor_add(acc[:, wi:wi + 1], acc[:, wi:wi + 1], w1t[:])
        nc.vector.drain()

    def wjob_finalize():
        psw1 = ws["psw1"]
        rddone = P.tick(nc.vector.engine_nop(), "ve")
        for wi in range(3):
            P.wdeps(nc.tensor, [rddone])
            tr = P.tick(nc.tensor.transpose(psw1[0:1, wi * 128:(wi + 1) * 128],
                                            acc[:, wi:wi + 1], idnf[:]), "te")
            P.wdeps(nc.vector, [tr])
            nc.vector.reduce_sum(w1t[0:1, 0:1], psw1[0:1, wi * 128:(wi + 1) * 128],
                                 axis=AX.X)
            nc.vector.drain()
            nc.vector.tensor_scalar(w1t[0:1, 0:1], w1t[0:1, 0:1],
                                    1.0 / float(II * E), EPS, AL.mult, AL.max)
            sc = P.tick(nc.vector.drain(), "ve")
            P.wdeps(nc.tensor, [sc])
            mm = P.tick(nc.tensor.matmul(psw1[0:128, 384 + wi:385 + wi], ones1[:],
                                         w1t[0:1, 0:1], start=True, stop=True), "te")
            P.wdeps(nc.vector, [mm])
            nc.vector.tensor_copy(mbar[:, wi:wi + 1], psw1[:, 384 + wi:385 + wi])
            nc.vector.drain()
            nc.vector.reciprocal(swi[:, wi:wi + 1], mbar[:, wi:wi + 1])
            rddone = P.tick(nc.vector.drain(), "ve")
        ws["findone"] = rddone
        ws["i"] = 0

    def wjob_tern():
        i = ws["i"]; ws["i"] += 1
        wi, srcw = wtiles[i]
        b = i % 2
        buf = w1b[:, b * 2048:(b + 1) * 2048]
        ld = P.dma(nc.gpsimd, buf, srcw, "wa" if b == 0 else "wb",
                   deps=[ws["free"][b]])
        P.wdeps(nc.vector, [ld, ws["findone"], ws["tfree"]])
        nc.vector.tensor_scalar(w2t[:], buf, swi[:, wi:wi + 1], 0.5,
                                AL.mult, AL.is_gt)
        nc.vector.drain()
        nc.vector.tensor_scalar(w2u[:], buf, swi[:, wi:wi + 1], -0.5,
                                AL.mult, AL.is_lt)
        ws["free"][b] = P.tick(nc.vector.drain(), "ve")
        P.wdeps(nc.vector, [ws["ofree"][b]])
        wi2, r, cb = wdst[i]
        ob = w2o[:, b * 2048:(b + 1) * 2048]
        if wi2 == 0:
            dst = tern_g[r * 128:(r + 1) * 128, :]
            nc.vector.tensor_tensor(ob, w2t[:], w2u[:], AL.subtract)
        elif wi2 == 1:
            dst = tern_u[r * 128:(r + 1) * 128, :]
            nc.vector.tensor_tensor(ob, w2t[:], w2u[:], AL.subtract)
        else:
            obp = bass.AP(w2o, b * 2048, [[2 * 2048, 128], [32, 64], [16, 2], [1, 16]])
            t1p = bass.AP(w2t, 0, [[2048, 128], [1, 64], [64, 2], [128, 16]])
            t2p = bass.AP(w2u, 0, [[2048, 128], [1, 64], [64, 2], [128, 16]])
            nc.vector.tensor_tensor(obp, t1p, t2p, AL.subtract)
            ws["tfree"] = td = P.tick(nc.vector.drain(), "ve")
            sem2 = "sa" if b == 0 else "sb"
            for dlt in range(2):
                dst = bass.AP(wds.tensor, r * 128 * II + cb * 16 + dlt * 64,
                              [[II, 128], [128, 64], [1, 16]])
                obh = bass.AP(w2o, b * 2048 + dlt * 16,
                              [[2 * 2048, 128], [32, 64], [1, 16]])
                P.dma(nc.gpsimd, dst, obh, sem2, deps=[td] if dlt == 0 else ())
            ws["ofree"][b] = P.mark(sem2)
            return
        ws["tfree"] = td = P.tick(nc.vector.drain(), "ve")
        ws["ofree"][b] = P.dma(nc.gpsimd, dst, ob, "sa" if b == 0 else "sb",
                               deps=[td])

    NW = len(wtiles)
    wdst = {}
    for k, (wi, _) in enumerate(wtiles):
        if wi == 0:
            wdst[k] = (0, k, 0)
        elif wi == 1:
            wdst[k] = (1, k - II // 128, 0)
        else:
            kk = k - 2 * (II // 128)
            wdst[k] = (2, kk // 4, kk % 4)

    wjobs = ([wjob_mean] * NW) + [wjob_finalize] + [wjob_tern] * NW

    def wpump(n=1):
        for _ in range(n):
            if wjobs:
                wjobs.pop(0)()

    NT = TH // 128
    with (
        nc.sbuf_tensor("xb1", [128, 2 * 2048], F32) as xb1,
        nc.sbuf_tensor("xnf", [128, 2048], F32) as xnf,
        nc.sbuf_tensor("qt1", [128, 2048], F32) as qt1,
        nc.sbuf_tensor("qi1", [128, 2048], mybir.dt.int32) as qi1,
        nc.sbuf_tensor("qd1", [128, 2048], F32) as qd1,
        nc.sbuf_tensor("qc1", [128, 2048], BF16) as qc1,
        nc.sbuf_tensor("qc2", [128, 2048], BF16) as qc2,
        nc.sbuf_tensor("xnbb", [128, 2 * 2048], BF16) as xnbb,
        nc.sbuf_tensor("xqb", [128, 2 * 2048], BF16) as xqb,
        nc.sbuf_tensor("ln", [128, 40], F32) as ln,
    ):
        xfree = [(), ()]
        bfree = [(), ()]
        qfree = [(), ()]
        prev_rs = ()
        for k in range(NT):
            b = k % 2
            xt = xb1[:, b * 2048:(b + 1) * 2048]
            ld = P.dma(nc.scalar, xt, xc[k * 128:(k + 1) * 128, :],
                       "la" if b == 0 else "lb", deps=[xfree[b]])
            P.wdeps(nc.vector, [ld])
            if prev_rs:
                P.wdeps(nc.vector, [prev_rs])
            for q in range(4):
                nc.vector.bn_stats(ln[:, 16 + 6 * q:16 + 6 * (q + 1)],
                                   xt[:, q * 512:(q + 1) * 512])
                nc.vector.drain()
            nc.vector.bn_aggr(ln[:, 6:8], ln[:, 16:40])
            st = P.tick(nc.vector.drain(), "ve")
            P.wdeps(nc.scalar, [st])
            nc.scalar.drain()
            nc.scalar.activation(ln[:, 8:9], ln[:, 7:8], AF.Ln, bias=EPS, scale=1.0)
            nc.scalar.drain()
            rs = prev_rs = P.tick(nc.scalar.activation(ln[:, 9:10], ln[:, 8:9], AF.Exp,
                                                       bias=0.0, scale=-0.5), "ae")
            P.wdeps(nc.vector, [rs, bfree[b]] if bfree[b] else [rs])
            nc.vector.tensor_scalar(xnbb[:, b * 2048:(b + 1) * 2048], xt,
                                    ln[:, 6:7], ln[:, 9:10], AL.subtract, AL.mult)
            bdone = P.tick(nc.vector.drain(), "ve")
            nc.vector.tensor_scalar(xnf[:], xt, ln[:, 6:7], ln[:, 9:10],
                                    AL.subtract, AL.mult)
            xfree[b] = P.tick(nc.vector.drain(), "ve")
            P.dma(nc.gpsimd, xnb_s[k * 128:(k + 1) * 128, :],
                  xnbb[:, b * 2048:(b + 1) * 2048],
                  "sa" if b == 0 else "sb", deps=[bdone])
            bfree[b] = P.mark("sa" if b == 0 else "sb")
            if k >= HALO // 128:
                nc.vector.reduce_max(ln[:, 10:11], xnf[:], axis=AX.X,
                                     apply_absolute_value=True)
                nc.vector.drain()
                nc.vector.tensor_scalar(ln[:, 11:12], ln[:, 10:11], EPS, 1.0 / 128.0,
                                        AL.max, AL.mult)
                nc.vector.drain()
                nc.vector.tensor_scalar(ln[:, 12:13], ln[:, 10:11], EPS, None, AL.max)
                nc.vector.drain()
                nc.vector.reciprocal(ln[:, 13:14], ln[:, 12:13])
                nc.vector.drain()
                nc.vector.tensor_scalar(ln[:, 14:15], ln[:, 13:14], 128.0, None,
                                        AL.mult)
                nc.vector.drain()
                nc.vector.tensor_scalar(qt1[:], xnf[:], ln[:, 14:15], None, AL.mult)
                nc.vector.drain()
                nc.vector.tensor_scalar(qi1[:], qt1[:], 1024.0, None, AL.add)
                nc.vector.drain()
                nc.vector.scalar_tensor_tensor(qd1[:], qt1[:], 1024.0, qi1[:],
                                               AL.add, AL.subtract)
                nc.vector.drain()
                nc.vector.tensor_scalar(qc1[:], qd1[:], 0.5, None, AL.is_gt)
                nc.vector.drain()
                nc.vector.tensor_scalar(qc2[:], qd1[:], 1.5, None, AL.is_gt)
                nc.vector.drain()
                nc.vector.tensor_scalar(qd1[:], qd1[:], -0.5, None, AL.is_lt)
                nc.vector.drain()
                nc.vector.tensor_sub(qc2[:], qc2[:], qd1[:])
                nc.vector.drain()
                nc.vector.scalar_tensor_tensor(qt1[:], qi1[:], -1024.0, qc1[:],
                                               AL.add, AL.add)
                nc.vector.drain()
                nc.vector.tensor_add(qt1[:], qt1[:], qc2[:])
                nc.vector.drain()
                nc.vector.tensor_scalar(qt1[:], qt1[:], -128.0, 127.0, AL.max, AL.min)
                nc.vector.drain()
                P.wdeps(nc.vector, [qfree[b]])
                nc.vector.tensor_scalar(xqb[:, b * 2048:(b + 1) * 2048], qt1[:],
                                        ln[:, 11:12], None, AL.mult)
                qdone = P.tick(nc.vector.drain(), "ve")
                kq = k - HALO // 128
                P.dma(nc.gpsimd, xnq_s[kq * 128:(kq + 1) * 128, :],
                      xqb[:, b * 2048:(b + 1) * 2048],
                      "sa" if b == 0 else "sb", deps=[qdone])
                qfree[b] = bfree[b] = P.mark("sa" if b == 0 else "sb")
            wpump(2)
    P.barrier()

    sweeps = []
    t0 = 0
    while t0 < TH:
        ts = min(1024, TH - t0)
        sweeps.append((t0, ts))
        t0 += ts
    with (
        nc.sbuf_tensor("xbs", [128, NE * 1024], BF16) as xbs,
        nc.sbuf_tensor("wsl", [128, NE * 1024], BF16) as wsl,
        nc.sbuf_tensor("pje", [128, 2 * 1024], BF16) as pje,
        nc.psum_tensor("psp", [128, 4 * 512], F32) as psp,
        nc.psum_tensor("psw1", [128, 512], F32) as psw1,
    ):
        ws["psw1"] = psw1
        xbs_free = ()
        wsl_free = ()
        pje_free = [(), ()]
        ps_free = [(), ()]
        for (st, ts) in sweeps:
            nsl = ts // 512
            for f in range(NE):
                P.dmat(nc.sync, xbs[:, f * 1024:f * 1024 + ts],
                       xnb_s[st:st + ts, f * 128:(f + 1) * 128], "xs",
                       deps=(xbs_free,) if f == 0 else ())
            xload = P.mark("xs")
            for vh in range(2):
                for f in range(NE):
                    P.dmat(nc.sync, wsl[:, f * 1024:(f + 1) * 1024],
                           wqkv_bf[4096 + vh * 1024:4096 + (vh + 1) * 1024,
                                   f * 128:(f + 1) * 128], "ws",
                           deps=(wsl_free, castdone) if f == 0 else ())
                wload = P.mark("ws")
                P.wdeps(nc.tensor, [xload, wload])
                for tb in range(ts // 128):
                    pb = 2 * (tb % 2)
                    P.wdeps(nc.tensor, [ps_free[tb % 2]] if ps_free[tb % 2] else [])
                    mi = None
                    for f in range(NE):
                        for es in range(2):
                            mi = nc.tensor.matmul(
                                psp[:, (pb + es) * 512:(pb + es + 1) * 512],
                                xbs[:, f * 1024 + tb * 128:f * 1024 + (tb + 1) * 128],
                                wsl[:, f * 1024 + es * 512:f * 1024 + (es + 1) * 512],
                                start=(f == 0), stop=(f == NE - 1))
                    mm = P.tick(mi, "te")
                    P.wdeps(nc.scalar, [mm, pje_free[tb % 2]] if pje_free[tb % 2] else [mm])
                    nc.scalar.drain()
                    ev = P.tick(nc.scalar.activation(
                        pje[:, (tb % 2) * 1024:(tb % 2 + 1) * 1024],
                        psp[:, pb * 512:(pb + 2) * 512], AF.Copy), "ae")
                    ps_free[tb % 2] = ev
                    pje_free[tb % 2] = P.dma(
                        nc.sync, vt_s[st + tb * 128:st + (tb + 1) * 128,
                                      vh * 1024:(vh + 1) * 1024],
                        pje[:, (tb % 2) * 1024:(tb % 2 + 1) * 1024],
                        "so" if tb % 2 == 0 else "sp", deps=[ev])
                    wpump(2)
                wsl_free = P.mark("te")
            for og in range(4):
                for f in range(NE):
                    P.dmat(nc.sync, wsl[:, f * 1024:(f + 1) * 1024],
                           wqkv_bf[og * 1024:(og + 1) * 1024, f * 128:(f + 1) * 128],
                           "ws", deps=(wsl_free, castdone) if f == 0 else ())
                wload = P.mark("ws")
                P.wdeps(nc.tensor, [wload])
                for ot in range(8):
                    pb = nsl * (ot % 2)
                    P.wdeps(nc.tensor, [ps_free[ot % 2]] if ps_free[ot % 2] else [])
                    mi = None
                    for f in range(NE):
                        for sl in range(nsl):
                            mi = nc.tensor.matmul(
                                psp[:, (pb + sl) * 512:(pb + sl + 1) * 512],
                                wsl[:, f * 1024 + ot * 128:f * 1024 + (ot + 1) * 128],
                                xbs[:, f * 1024 + sl * 512:f * 1024 + (sl + 1) * 512],
                                start=(f == 0), stop=(f == NE - 1))
                    mm = P.tick(mi, "te")
                    P.wdeps(nc.scalar, [mm, pje_free[ot % 2]] if pje_free[ot % 2] else [mm])
                    nc.scalar.drain()
                    ev = P.tick(nc.scalar.activation(
                        pje[:, (ot % 2) * 1024:(ot % 2) * 1024 + ts],
                        psp[:, pb * 512:pb * 512 + ts], AF.Copy), "ae")
                    ps_free[ot % 2] = ev
                    orow = og * 1024 + ot * 128
                    dst = q_s if orow < 2048 else k_s
                    orow2 = orow % 2048
                    pje_free[ot % 2] = P.dma(
                        nc.sync, dst[orow2:orow2 + 128, st:st + ts],
                        pje[:, (ot % 2) * 1024:(ot % 2) * 1024 + ts],
                        "so" if ot % 2 == 0 else "sp", deps=[ev])
                    wpump(2)
                wsl_free = P.mark("te")
            xbs_free = P.mark("te")
    P.barrier()

    with (
        nc.sbuf_tensor("Kw", [128, NE * 1024], BF16) as Kw,
        nc.sbuf_tensor("VTw", [128, 8 * 2048], BF16) as VTw,
        nc.sbuf_tensor("Qc", [128, NE * 512], BF16) as Qc,
        nc.sbuf_tensor("Pe", [128, 4 * 640], BF16) as Pe,
        nc.sbuf_tensor("PT", [128, 8 * 512], BF16) as PT,
        nc.sbuf_tensor("Dt", [128, 256], BF16) as Dt,
        nc.sbuf_tensor("aveb", [128, 2 * 512], BF16) as aveb,
        nc.sbuf_tensor("sm", [128, 8], F32) as sm,
        nc.psum_tensor("psS", [128, 2 * 1024], F32) as psS,
        nc.psum_tensor("psPT", [128, 2 * 512], F32) as psPT,
        nc.psum_tensor("psAV", [128, 2 * 512], F32) as psAV,
    ):
        ldfree = ()
        s_free = [(), ()]
        pt_free = [(), ()]
        av_free = [(), ()]
        ab_free = [(), ()]
        pe_free = ()
        dt_free = [(), ()]
        for n in range(NCH):
            kw = n * 512
            qg = HALO + n * 512
            P.wdeps(nc.scalar, [ldfree])
            for hb in range(NE):
                P.dma(nc.scalar, Kw[:, hb * 1024:(hb + 1) * 1024],
                      k_s[hb * 128:(hb + 1) * 128, kw:kw + 1024], "ka")
            for kb in range(8):
                P.dma(nc.scalar, VTw[:, kb * 2048:(kb + 1) * 2048],
                      vt_s[kw + kb * 128:kw + (kb + 1) * 128, :], "ka")
            for hb in range(NE):
                P.dma(nc.scalar, Qc[:, hb * 512:(hb + 1) * 512],
                      q_s[hb * 128:(hb + 1) * 128, qg:qg + 512], "ka")
            ld = P.mark("ka")
            P.wdeps(nc.tensor, [ld])
            P.wdeps(nc.vector, [ld])

            def p2b_scores(j):
                h, s = j // 4, j % 4
                pp = j % 2
                ps = psS[:, pp * 1024:pp * 1024 + 640]
                P.wdeps(nc.tensor, [s_free[pp]])
                nc.tensor.matmul(ps[:, 0:512],
                                 Qc[:, h * 512 + s * 128:h * 512 + (s + 1) * 128],
                                 Kw[:, h * 1024 + s * 128:h * 1024 + s * 128 + 512],
                                 start=True, stop=True)
                return P.tick(nc.tensor.matmul(
                    ps[:, 512:640],
                    Qc[:, h * 512 + s * 128:h * 512 + (s + 1) * 128],
                    Kw[:, h * 1024 + s * 128 + 512:h * 1024 + s * 128 + 640],
                    start=True, stop=True), "te")

            def p2b_softmax(j, sc2):
                h, s = j // 4, j % 4
                pp = j % 2
                ps = psS[:, pp * 1024:pp * 1024 + 640]
                P.wdeps(nc.vector, [sc2])
                if n == 0:
                    nc.vector.tensor_add(ps, ps, maskz_t[:, s * 640:(s + 1) * 640])
                else:
                    nc.vector.tensor_add(ps[:, 0:128], ps[:, 0:128],
                                         maskb_t[:, 0:128])
                    nc.vector.drain()
                    nc.vector.tensor_add(ps[:, 512:640], ps[:, 512:640],
                                         maskb_t[:, 128:256])
                nc.vector.drain()
                nc.vector.reduce_max(sm[:, pp * 4:pp * 4 + 1], ps, axis=AX.X)
                nc.vector.drain()
                nc.vector.tensor_scalar(sm[:, pp * 4 + 1:pp * 4 + 2],
                                        sm[:, pp * 4:pp * 4 + 1], -SM_SCALE, None,
                                        AL.mult)
                mx = P.tick(nc.vector.drain(), "ve")
                P.wdeps(nc.scalar, [mx, pe_free])
                nc.scalar.drain()
                s_free[pp] = ex = P.tick(nc.scalar.activation(
                    Pe[:, s * 640:(s + 1) * 640], ps, AF.Exp,
                    bias=sm[:, pp * 4 + 1:pp * 4 + 2], scale=SM_SCALE,
                    accum_out=sm[:, pp * 4 + 2:pp * 4 + 3]), "ae")
                P.wdeps(nc.vector, [ex, dt_free[pp]])
                nc.vector.reciprocal(sm[:, pp * 4 + 3:pp * 4 + 4],
                                     sm[:, pp * 4 + 2:pp * 4 + 3])
                nc.vector.drain()
                nc.vector.tensor_scalar(Dt[:, pp * 128:(pp + 1) * 128], idnb[:],
                                        sm[:, pp * 4 + 3:pp * 4 + 4], None, AL.mult)
                return P.tick(nc.vector.drain(), "ve")

            def p2b_transposes(j, dv):
                h, s = j // 4, j % 4
                pp = j % 2
                P.wdeps(nc.tensor, [dv])
                for jb in range(5):
                    c = s + jb
                    P.wdeps(nc.tensor, [pt_free[jb % 2]])
                    tr = P.tick(nc.tensor.matmul(
                        psPT[:, (jb % 2) * 512:(jb % 2) * 512 + 128],
                        Pe[:, s * 640 + jb * 128:s * 640 + (jb + 1) * 128],
                        Dt[:, pp * 128:(pp + 1) * 128], start=True, stop=True), "te")
                    if jb == 4:
                        dt_free[pp] = tr
                    P.wdeps(nc.vector, [tr])
                    nc.vector.tensor_copy(
                        PT[:, c * 512 + s * 128:c * 512 + (s + 1) * 128],
                        psPT[:, (jb % 2) * 512:(jb % 2) * 512 + 128])
                    pt_free[jb % 2] = P.tick(nc.vector.drain(), "ve")

            def p2b_av(h):
                nonlocal pe_free
                hp = h % 2
                pa = psAV[:, hp * 512:(hp + 1) * 512]
                P.wdeps(nc.tensor, [av_free[hp], P.mark("ve")])
                mi = None
                corder = [4] + [c for c in range(8) if c != 4]
                for ci_, c in enumerate(corder):
                    a0 = max(0, c - 4) * 128
                    a1 = (min(3, c) + 1) * 128
                    mi = nc.tensor.matmul(pa[:, a0:a1],
                                          VTw[:, c * 2048 + h * 128:c * 2048 + (h + 1) * 128],
                                          PT[:, c * 512 + a0:c * 512 + a1],
                                          start=(ci_ == 0), stop=(ci_ == 7))
                pe_free = av = P.tick(mi, "te")
                P.wdeps(nc.scalar, [av, ab_free[hp]])
                nc.scalar.drain()
                av_free[hp] = ev = P.tick(nc.scalar.activation(
                    aveb[:, hp * 512:(hp + 1) * 512], pa, AF.Copy), "ae")
                ab_free[hp] = P.dma(nc.sync,
                                    att_s[h * 128:(h + 1) * 128, n * 512:(n + 1) * 512],
                                    aveb[:, hp * 512:(hp + 1) * 512],
                                    "so" if hp == 0 else "sp", deps=[ev])

            prev = None
            for j in range(NH * 4 + 1):
                if j < NH * 4:
                    sc2 = p2b_scores(j)
                if prev is not None:
                    jp, sc2p = prev
                    dv = p2b_softmax(jp, sc2p)
                    p2b_transposes(jp, dv)
                    if jp % 4 == 3:
                        p2b_av(jp // 4)
                        wpump(1)
                prev = (j, sc2) if j < NH * 4 else None
            ldfree = P.mark("te")
        while wjobs:
            wpump(1)
    wst.close()
    P.barrier()

    with (
        nc.sbuf_tensor("xqs", [128, NE * 2048], BF16) as xqs,
        nc.sbuf_tensor("tsl", [128, 2 * NE * 1024], BF16) as tsl,
        nc.sbuf_tensor("sg", [128, 8 * 2048], BF16) as sg,
        nc.sbuf_tensor("ie", [128, 2 * 2048], BF16) as ie,
        nc.psum_tensor("psf", [128, 8 * 512], F32) as psf,
    ):
        TSL = max(1, T // 512)
        for f in range(NE):
            P.dmat(nc.sync, xqs[:, f * 2048:f * 2048 + T],
                   xnq_s[0:T, f * 128:(f + 1) * 128], "xs")
        xqload = P.mark("xs")
        passes = []
        for og in range(II // 1024):
            passes.append(("g", og))
            passes.append(("u", og))
        slab_ready = {}
        slab_free = [(), ()]

        def load_slab(i):
            w, og = passes[i]
            b = i % 2
            src = tern_g if w == "g" else tern_u
            sname = "ws" if b == 0 else "wt"
            for f in range(NE):
                P.dmat(nc.sync, tsl[:, b * NE * 1024 + f * 1024:b * NE * 1024 + (f + 1) * 1024],
                       src[og * 1024:(og + 1) * 1024, f * 128:(f + 1) * 128], sname,
                       deps=(slab_free[b],) if f == 0 else ())
            slab_ready[i] = P.mark(sname)

        load_slab(0)
        load_slab(1)
        ps_free = [(), ()]
        sg_done = [()] * 8
        sg_free = [()] * 8
        ie_free = [(), ()]
        for i, (w, og) in enumerate(passes):
            b = i % 2
            P.wdeps(nc.tensor, [slab_ready[i], xqload])
            for ot in range(8):
                pb = (ot % 2) * 4
                P.wdeps(nc.tensor, [ps_free[ot % 2]] if ps_free[ot % 2] else [])
                mi = None
                for f in range(NE):
                    for sl in range(TSL):
                        mi = nc.tensor.matmul(
                            psf[:, (pb + sl) * 512:(pb + sl + 1) * 512],
                            tsl[:, b * NE * 1024 + f * 1024 + ot * 128:b * NE * 1024 + f * 1024 + (ot + 1) * 128],
                            xqs[:, f * 2048 + sl * 512:f * 2048 + (sl + 1) * 512],
                            start=(f == 0), stop=(f == NE - 1))
                mm = P.tick(mi, "te")
                if w == "g":
                    P.wdeps(nc.scalar, [mm])
                    if sg_free[ot]:
                        P.wdeps(nc.scalar, [sg_free[ot]])
                    nc.scalar.drain()
                    ps_free[ot % 2] = sg_done[ot] = P.tick(
                        nc.scalar.activation(sg[:, ot * 2048:ot * 2048 + T],
                                             psf[:, pb * 512:pb * 512 + T],
                                             AF.Silu, bias=0.0,
                                             scale=mbar[:, 0:1]), "ae")
                else:
                    P.wdeps(nc.vector, [mm, sg_done[ot]])
                    if ie_free[ot % 2]:
                        P.wdeps(nc.vector, [ie_free[ot % 2]])
                    nc.vector.scalar_tensor_tensor(
                        ie[:, (ot % 2) * 2048:(ot % 2) * 2048 + T],
                        psf[:, pb * 512:pb * 512 + T],
                        mbar[:, 1:2], sg[:, ot * 2048:ot * 2048 + T],
                        AL.mult, AL.mult)
                    ps_free[ot % 2] = iv = sg_free[ot] = P.tick(nc.vector.drain(), "ve")
                    ie_free[ot % 2] = P.dma(
                        nc.gpsimd,
                        inter_s[og * 1024 + ot * 128:og * 1024 + (ot + 1) * 128, 0:T],
                        ie[:, (ot % 2) * 2048:(ot % 2) * 2048 + T],
                        "sa" if ot % 2 == 0 else "sb", deps=[iv])
            slab_free[b] = P.mark("te")
            if i + 2 < len(passes):
                load_slab(i + 2)
    P.barrier()

    NTILE = max(1, T // 1024)
    TT = T // NTILE
    NTG = TT // 128
    with (
        nc.sbuf_tensor("bufA", [128, NI * TT], BF16) as bufA,
        nc.sbuf_tensor("zz", [128, 2 * 1024], BF16) as zz,
        nc.sbuf_tensor("Mx", [128, TT], F32) as Mx,
        nc.sbuf_tensor("syc", [128, 4 * 16], F32) as syc,
        nc.sbuf_tensor("syr", [1, 2 * TT], F32) as syr,
        nc.sbuf_tensor("Sb", [128, 2 * TT], F32) as Sb,
        nc.sbuf_tensor("qtm", [128, TT], F32) as qtm,
        nc.sbuf_tensor("qi2", [128, TT], mybir.dt.int32) as qi2,
        nc.sbuf_tensor("qd2", [128, TT], F32) as qd2,
        nc.sbuf_tensor("qc3", [128, TT], BF16) as qc3,
        nc.sbuf_tensor("qc4", [128, TT], BF16) as qc4,
        nc.sbuf_tensor("wdsl", [128, 3 * 1024], BF16) as wdsl,
        nc.sbuf_tensor("atl", [128, 8 * 512], BF16) as atl,
        nc.sbuf_tensor("xr", [128, 2 * 512], F32) as xr,
        nc.sbuf_tensor("fo", [128, 2 * 512], F32) as fo,
    ):
        buf_free = ()
        atl_free = ()
        wfree = [(), (), ()]
        x_free = [(), ()]
        f_free = [(), ()]
        for tt in range(NTILE):
            tof = tt * TT
            P.wdeps(nc.scalar, [buf_free])
            for ci in range(NI):
                P.dma(nc.scalar, bufA[:, ci * TT:(ci + 1) * TT],
                      inter_s[ci * 128:(ci + 1) * 128, tof:tof + TT], "ka")
            ild = P.mark("ka")
            P.wdeps(nc.tensor, [ild])
            with nc.psum_tensor(f"psZ{tt}", [128, 2 * 512], F32) as psZ:
                z_free = [(), ()]
                for ci in range(NI):
                    for th in range(TT // 512):
                        j = ci * (TT // 512) + th
                        pz = j % 2
                        P.wdeps(nc.tensor, [z_free[pz]] if z_free[pz] else [])
                        mm = P.tick(nc.tensor.matmul(
                            psZ[:, pz * 512:(pz + 1) * 512], h128[:],
                            bufA[:, ci * TT + th * 512:ci * TT + (th + 1) * 512],
                            start=True, stop=True), "te")
                        P.wdeps(nc.scalar, [mm])
                        nc.scalar.drain()
                        z_free[pz] = P.tick(nc.scalar.activation(
                            bufA[:, ci * TT + th * 512:ci * TT + (th + 1) * 512],
                            psZ[:, pz * 512:(pz + 1) * 512], AF.Copy), "ae")
                s1done = P.mark("ae")
            with (
                nc.psum_tensor(f"psT{tt}", [128, 2 * 1024], BF16) as psT,
                nc.psum_tensor(f"psY{tt}", [128, 2 * 1024], F32) as psY,
            ):
                P.wdeps(nc.tensor, [s1done])
                t_free = [(), ()]
                y_free = [(), ()]
                zz_free = [(), ()]
                for g in range(TT // 16):
                    gp = g % 2
                    P.wdeps(nc.tensor, [t_free[gp]])
                    mi = None
                    for tl in range(16):
                        t = g * 16 + tl
                        for d in range(2):
                            mi = nc.tensor.transpose(
                                psT[d * 64:(d + 1) * 64,
                                    gp * 1024 + tl * 64:gp * 1024 + (tl + 1) * 64],
                                bufA[d * 64:(d + 1) * 64, t::TT],
                                idnb[d * 64:(d + 1) * 64, d * 64:(d + 1) * 64])
                    tr = P.tick(mi, "te")
                    P.wdeps(nc.vector, [tr, zz_free[gp]])
                    nc.vector.tensor_copy(zz[:, gp * 1024:(gp + 1) * 1024],
                                          psT[:, gp * 1024:(gp + 1) * 1024])
                    t_free[gp] = zc = P.tick(nc.vector.drain(), "ve")
                    P.wdeps(nc.tensor, [zc, y_free[gp]])
                    nc.tensor.matmul(psY[:, gp * 1024:gp * 1024 + 512], bd64[:],
                                     zz[:, gp * 1024:gp * 1024 + 512],
                                     start=True, stop=True)
                    s2 = P.tick(nc.tensor.matmul(psY[:, gp * 1024 + 512:(gp + 1) * 1024],
                                                 bd64[:],
                                                 zz[:, gp * 1024 + 512:(gp + 1) * 1024],
                                                 start=True, stop=True), "te")
                    zz_free[gp] = s2
                    P.wdeps(nc.vector, [s2])
                    nc.vector.reduce_max(
                        Mx[:, g * 16:(g + 1) * 16],
                        psY[:, gp * 1024:(gp + 1) * 1024].rearrange("p (a b) -> p a b", a=16),
                        axis=AX.X, apply_absolute_value=True)
                    m8 = P.tick(nc.vector.drain(), "ve")
                    P.wdeps(nc.scalar, [m8])
                    nc.scalar.drain()
                    dst = bass.AP(bufA, g * 16, [[NI * TT, 128], [1, 16], [TT, 64]])
                    src2 = psY[:, gp * 1024:(gp + 1) * 1024].rearrange("p (a b) -> p a b", a=16)
                    y_free[gp] = P.tick(nc.scalar.activation(dst, src2, AF.Copy), "ae")
                hdone_ae = P.mark("ae")
                hdone_ve = P.mark("ve")
            with nc.psum_tensor(f"psA{tt}", [128, 2 * 512], F32) as psA:
                P.wdeps(nc.tensor, [hdone_ae, hdone_ve])
                a_free = [(), ()]
                for tg in range(NTG):
                    ap = tg % 2
                    P.wdeps(nc.tensor, [a_free[ap]] if a_free[ap] else [])
                    tr = P.tick(nc.tensor.transpose(
                        psA[:, ap * 512:ap * 512 + 128],
                        Mx[:, tg * 128:(tg + 1) * 128], idnf[:]), "te")
                    P.wdeps(nc.vector, [tr])
                    nc.vector.reduce_max(syc[:, tg:tg + 1],
                                         psA[:, ap * 512:ap * 512 + 128], axis=AX.X)
                    a_free[ap] = P.tick(nc.vector.drain(), "ve")
                for tg in range(NTG):
                    nc.vector.tensor_scalar(syc[:, 16 + tg:17 + tg], syc[:, tg:tg + 1],
                                            NRM, EPS, AL.mult, AL.max)
                    nc.vector.drain()
                    nc.vector.tensor_scalar(syc[:, 32 + tg:33 + tg],
                                            syc[:, 16 + tg:17 + tg], 1.0 / 128.0,
                                            None, AL.mult)
                    nc.vector.drain()
                    nc.vector.reciprocal(syc[:, 48 + tg:49 + tg],
                                         syc[:, 16 + tg:17 + tg])
                    nc.vector.drain()
                scdone = P.tick(nc.vector.engine_nop(), "ve")
                P.wdeps(nc.tensor, [scdone])
                for tg in range(NTG):
                    for half, col in ((0, 48), (1, 32)):
                        ap = tg % 2
                        P.wdeps(nc.tensor, [a_free[ap]] if a_free[ap] else [])
                        tr = P.tick(nc.tensor.transpose(
                            psA[0:1, ap * 512:ap * 512 + 128],
                            syc[:, col + tg:col + tg + 1], idnf[:]), "te")
                        P.wdeps(nc.vector, [tr])
                        nc.vector.tensor_copy(
                            syr[0:1, half * TT + tg * 128:half * TT + (tg + 1) * 128],
                            psA[0:1, ap * 512:ap * 512 + 128])
                        a_free[ap] = P.tick(nc.vector.drain(), "ve")
                rdone = P.mark("ve")
                P.wdeps(nc.tensor, [rdone])
                for half in range(2):
                    for sl in range(TT // 512):
                        ap = sl % 2
                        P.wdeps(nc.tensor, [a_free[ap]] if a_free[ap] else [])
                        bm = P.tick(nc.tensor.matmul(
                            psA[:, ap * 512:(ap + 1) * 512], ones1[:],
                            syr[0:1, half * TT + sl * 512:half * TT + (sl + 1) * 512],
                            start=True, stop=True), "te")
                        P.wdeps(nc.vector, [bm])
                        nc.vector.tensor_copy(
                            Sb[:, half * TT + sl * 512:half * TT + (sl + 1) * 512],
                            psA[:, ap * 512:(ap + 1) * 512])
                        a_free[ap] = P.tick(nc.vector.drain(), "ve")
            if dbg and tt == 0:
                P.wdeps(nc.gpsimd, [hdone_ae, P.mark("ve")])
                for u in range(NI):
                    P.dma(nc.gpsimd, dbg_h[:, u * TT:(u + 1) * TT],
                          bufA[:, u * TT:(u + 1) * TT], "wc")
                P.dma(nc.gpsimd, dbg_m[:, 0:TT], Mx[:, 0:TT], "wc")
                P.dma(nc.gpsimd, dbg_m[:, TT:2 * TT], Sb[:, 0:TT], "wc")
                P.dma(nc.gpsimd, dbg_m[:, 2 * TT:3 * TT], Sb[:, TT:2 * TT], "wc")
                dhd = P.mark("wc")
                P.wdeps(nc.vector, [dhd])
            P.wdeps(nc.vector, [hdone_ae])
            qdone_u = []
            for u in range(NI):
                nc.vector.scalar_tensor_tensor(qtm[:, 0:TT], bufA[:, u * TT:(u + 1) * TT],
                                               128.0 * NRM, Sb[:, 0:TT], AL.mult, AL.mult)
                nc.vector.drain()
                nc.vector.tensor_scalar(qi2[:], qtm[:, 0:TT], 1024.0, None, AL.add)
                nc.vector.drain()
                nc.vector.scalar_tensor_tensor(qd2[:], qtm[:, 0:TT], 1024.0, qi2[:],
                                               AL.add, AL.subtract)
                nc.vector.drain()
                nc.vector.tensor_scalar(qc3[:], qd2[:], 0.5, None, AL.is_gt)
                nc.vector.drain()
                nc.vector.tensor_scalar(qc4[:], qd2[:], 1.5, None, AL.is_gt)
                nc.vector.drain()
                nc.vector.tensor_scalar(qd2[:], qd2[:], -0.5, None, AL.is_lt)
                nc.vector.drain()
                nc.vector.tensor_sub(qc4[:], qc4[:], qd2[:])
                nc.vector.drain()
                nc.vector.scalar_tensor_tensor(qtm[:, 0:TT], qi2[:], -1024.0, qc3[:],
                                               AL.add, AL.add)
                nc.vector.drain()
                nc.vector.tensor_add(qtm[:, 0:TT], qtm[:, 0:TT], qc4[:])
                nc.vector.drain()
                nc.vector.tensor_scalar(qtm[:, 0:TT], qtm[:, 0:TT], -128.0, 127.0,
                                        AL.max, AL.min)
                nc.vector.drain()
                nc.vector.scalar_tensor_tensor(bufA[:, u * TT:(u + 1) * TT], qtm[:, 0:TT],
                                               mbar[:, 2:3], Sb[:, TT:2 * TT],
                                               AL.mult, AL.mult)
                qdone_u.append(P.tick(nc.vector.drain(), "ve"))
            qdone = qdone_u[-1]
            if dbg and tt == 0:
                P.wdeps(nc.gpsimd, [qdone])
                for u in range(NI):
                    P.dma(nc.gpsimd, dbg_yq[:, u * TT:(u + 1) * TT],
                          bufA[:, u * TT:(u + 1) * TT], "wc")
                qdone = P.mark("wc")
                P.wdeps(nc.vector, [qdone])
                qdone = P.tick(nc.vector.engine_nop(), "ve")
            with nc.psum_tensor(f"psD{tt}", [128, 8 * 512], F32) as psD:
                st = {"d_free": (), "atl_free": atl_free}

                def f2b_mms(th, eh, qdone, ald):
                    P.wdeps(nc.scalar, [st["atl_free"]])
                    for ec in range(8):
                        P.dma(nc.scalar, atl[:, ec * 512:(ec + 1) * 512],
                              att_s[ec * 128:(ec + 1) * 128,
                                    tof + th * 512:tof + (th + 1) * 512], "kb")
                    ald = P.mark("kb")
                    ald2 = ()
                    sl_ready = [P.dmat(nc.sync, wdsl[:, 0:1024],
                                       wds[eh * 1024:(eh + 1) * 1024, 0:128], "ws",
                                       deps=[wfree[0]]),
                                P.dmat(nc.sync, wdsl[:, 1024:2048],
                                       wds[eh * 1024:(eh + 1) * 1024, 128:256], "wt",
                                       deps=[wfree[1]])]
                    P.wdeps(nc.tensor, [st["d_free"]])
                    for u in range(NI + NE):
                        bs = u % 3
                        P.wdeps(nc.tensor, [sl_ready[u]])
                        if u < NI:
                            P.wdeps(nc.tensor, [qdone_u[u]])
                        elif u == NI:
                            P.wdeps(nc.tensor, [ald])
                        elif u == NI + 8:
                            P.wdeps(nc.tensor, [ald2])
                        mi = None
                        for tg in range(4):
                            for es in range(2):
                                if u < NI:
                                    lhs = bufA[:, u * TT + th * 512 + tg * 128:
                                               u * TT + th * 512 + (tg + 1) * 128]
                                else:
                                    ecl = (u - NI) % 8
                                    lhs = atl[:, ecl * 512 + tg * 128:
                                              ecl * 512 + (tg + 1) * 128]
                                mi = nc.tensor.matmul(
                                    psD[:, (tg * 2 + es) * 512:(tg * 2 + es + 1) * 512],
                                    lhs,
                                    wdsl[:, bs * 1024 + es * 512:bs * 1024 + (es + 1) * 512],
                                    start=(u == 0), stop=(u == NI + NE - 1))
                        wfree[bs] = P.tick(mi, "te")
                        if u == NI + 7:
                            for ec in range(8, NE):
                                P.dma(nc.scalar, atl[:, (ec - 8) * 512:(ec - 7) * 512],
                                      att_s[ec * 128:(ec + 1) * 128,
                                            tof + th * 512:tof + (th + 1) * 512], "kb",
                                      deps=[wfree[bs]] if ec == 8 else ())
                            ald2 = P.mark("kb")
                        if u + 2 < NI + NE:
                            nb = (u + 2) % 3
                            nname = ("ws", "wt", "wu")[nb]
                            if u + 2 < NI:
                                srcw = wds[eh * 1024:(eh + 1) * 1024,
                                           (u + 2) * 128:(u + 3) * 128]
                            else:
                                ec2 = u + 2 - NI
                                srcw = wout_bf[eh * 1024:(eh + 1) * 1024,
                                               ec2 * 128:(ec2 + 1) * 128]
                            P.dmat(nc.sync, wdsl[:, nb * 1024:(nb + 1) * 1024],
                                   srcw, nname, deps=[wfree[nb]])
                            sl_ready.append(P.mark(nname))
                    return P.mark("te")

                def f2b_evac(th, eh, dmm, tof):
                    for tg in range(4):
                        for es in range(2):
                            bp = es % 2
                            gr = HALO + tof + th * 512 + tg * 128
                            xl = P.dma(nc.scalar, xr[:, bp * 512:(bp + 1) * 512],
                                       xc[gr:gr + 128,
                                          eh * 1024 + es * 512:eh * 1024 + (es + 1) * 512],
                                       "xa" if bp == 0 else "xb",
                                       deps=[x_free[bp]])
                            P.wdeps(nc.vector, [dmm, xl, f_free[bp]])
                            nc.vector.tensor_add(
                                fo[:, bp * 512:(bp + 1) * 512],
                                psD[:, (tg * 2 + es) * 512:(tg * 2 + es + 1) * 512],
                                xr[:, bp * 512:(bp + 1) * 512])
                            x_free[bp] = fv = P.tick(nc.vector.drain(), "ve")
                            f_free[bp] = P.dma(
                                nc.sync,
                                out_c[tof + th * 512 + tg * 128:tof + th * 512 + (tg + 1) * 128,
                                      eh * 1024 + es * 512:eh * 1024 + (es + 1) * 512],
                                fo[:, bp * 512:(bp + 1) * 512],
                                "so" if bp == 0 else "sp", deps=[fv])
                    st["d_free"] = P.mark("ve")

                for th in range(TT // 512):
                    for eh in range(2):
                        dmm = f2b_mms(th, eh, qdone, ())
                        f2b_evac(th, eh, dmm, tof)
                    st["atl_free"] = P.mark("te")
                atl_free = st["atl_free"]
            buf_free = P.mark("te")
    for s, eng in P.dmaeng.items():
        if P.cnt[s] > 0:
            eng.wait_ge(P.sem[s], P.cnt[s])
    P.close()
    return nc


def _host_inputs(x, T):
    B, S, _ = x.shape
    percore = []
    npc = S // T
    for c in range(8):
        b, pos = c // npc, (c % npc) * T
        xcv = np.zeros((T + HALO, E), np.float32)
        lo = pos - HALO
        if lo < 0:
            xcv[HALO:, :] = x[b, pos:pos + T]
        else:
            xcv[:, :] = x[b, lo:pos + T]
        mz = np.zeros((4, 128, 640), np.float32)
        ppv = np.arange(128)[:, None]
        for s in range(4):
            jlv = np.arange(640)[None, :]
            ok = (jlv >= ppv) & (jlv <= ppv + 512)
            if pos == 0:
                ok &= (s * 128 + jlv) >= HALO
            mz[s] = np.where(ok, 0.0, NEGM)
        percore.append({"xc": xcv, "maskz": mz})
    return percore


def kernel(L0_attn_x, L0_attn_w_qkv, L0_attn_w_out,
           L0_ffn_w_gate, L0_ffn_w_up, L0_ffn_w_down):
    x = np.ascontiguousarray(np.asarray(L0_attn_x, dtype=np.float32))
    B, S, _ = x.shape
    T = (B * S) // 8
    nc = build_nc(T=T)
    w = {
        "w_qkv": np.ascontiguousarray(L0_attn_w_qkv, dtype=np.float32),
        "w_out": np.ascontiguousarray(L0_attn_w_out, dtype=np.float32),
        "w_gate": np.ascontiguousarray(L0_ffn_w_gate, dtype=np.float32),
        "w_up": np.ascontiguousarray(L0_ffn_w_up, dtype=np.float32),
        "w_down": np.ascontiguousarray(L0_ffn_w_down, dtype=np.float32),
    }
    in_maps = []
    for pc in _host_inputs(x, T):
        m = dict(w)
        m.update(pc)
        in_maps.append(m)
    res = run_bass_kernel_spmd(nc, in_maps, core_ids=list(range(8)))
    out = np.empty((B, S, E), np.float32)
    npc = S // T
    for c in range(8):
        b, pos = c // npc, (c % npc) * T
        out[b, pos:pos + T] = res.results[c]["out_c"]
    return out


# version 14 (final)
# speedup vs baseline: 1.1154x; 1.0917x over previous
import numpy as np
import ml_dtypes
import concourse.bass as bass
import concourse.mybir as mybir
from concourse.bass_utils import run_bass_kernel_spmd

F32 = mybir.dt.float32
BF16 = mybir.dt.bfloat16
AL = mybir.AluOpType
AF = mybir.ActivationFunctionType
AX = mybir.AxisListType

E = 2048
NH = 16
DH = 128
O3 = 6144
II = 8192
CH = 512
NE = E // 128
NI = II // 128

MAGIC = 12582912.0
NEGM = -1e30
SM_SCALE = 1.0 / float(np.sqrt(128.0))
NRM = float(np.float32(1.0) / np.float32(np.sqrt(8192.0)))
EPS = 1e-5
HALO = 512


def _had(n):
    i = np.arange(n)
    pc = np.array([bin(v).count("1") for v in range(n)])
    m = pc[i[:, None] & i[None, :]]
    return np.where(m % 2 == 0, 1.0, -1.0).astype(np.float32)


class Prog:

    def __init__(self, nc):
        self.nc = nc
        self.cm = []
        self.sem = {}
        self.cnt = {}
        self.dmaeng = {}

    def enter(self, cm):
        v = cm.__enter__()
        self.cm.append(cm)
        return v

    def close(self):
        for cm in reversed(self.cm):
            cm.__exit__(None, None, None)
        self.cm = []

    def mksem(self, *names):
        for n in names:
            self.sem[n] = self.enter(self.nc.semaphore(n))
            self.cnt[n] = 0

    def mkdma(self, eng, *names):
        for n in names:
            self.mksem(n)
            self.dmaeng[n] = eng

    def dma(self, eng, dst, src, s, deps=()):
        self.wdeps(eng, deps)
        eng.dma_start(dst, src).then_inc(self.sem[s], 16)
        self.cnt[s] += 16
        return (s, self.cnt[s])

    def dmat(self, eng, dst, src, s, deps=()):
        self.wdeps(eng, deps)
        eng.dma_start_transpose(dst, src).then_inc(self.sem[s], 16)
        self.cnt[s] += 16
        return (s, self.cnt[s])

    def tick(self, instr, s, k=1):
        instr.then_inc(self.sem[s], k)
        self.cnt[s] += k
        return (s, self.cnt[s])

    def mark(self, s):
        return (s, self.cnt[s])

    def wdeps(self, eng, deps):
        for d in deps:
            if not d:
                continue
            (s, v) = d
            if v > 0:
                eng.wait_ge(self.sem[s], v)

    def barrier(self):
        for s, eng in self.dmaeng.items():
            if self.cnt[s] > 0:
                eng.wait_ge(self.sem[s], self.cnt[s])
        self.nc.all_engine_barrier()


def build_nc(T=2048, dbg=False):
    TH = T + HALO
    NCH = T // CH
    nc = bass.Bass("TRN2", target_bir_lowering=False)

    xc = nc.dram_tensor("xc", [TH, E], F32, kind="ExternalInput").ap()
    w_qkv = nc.dram_tensor("w_qkv", [O3, E], F32, kind="ExternalInput").ap()
    w_out = nc.dram_tensor("w_out", [E, E], F32, kind="ExternalInput").ap()
    w_gate = nc.dram_tensor("w_gate", [II, E], F32, kind="ExternalInput").ap()
    w_up = nc.dram_tensor("w_up", [II, E], F32, kind="ExternalInput").ap()
    w_down = nc.dram_tensor("w_down", [E, II], F32, kind="ExternalInput").ap()
    maskz = nc.dram_tensor("maskz", [4, 128, 640], F32, kind="ExternalInput").ap()
    out_c = nc.dram_tensor("out_c", [T, E], F32, kind="ExternalOutput").ap()

    wqkv_bf = nc.dram_tensor("wqkv_bf", [O3, E], BF16).ap()
    wout_bf = nc.dram_tensor("wout_bf", [E, E], BF16).ap()
    tern_g = nc.dram_tensor("tern_g", [II, E], BF16).ap()
    tern_u = nc.dram_tensor("tern_u", [II, E], BF16).ap()
    wds = nc.dram_tensor("wds", [E, II], BF16).ap()
    xnb_s = nc.dram_tensor("xnb_s", [TH, E], BF16).ap()
    xnq_s = nc.dram_tensor("xnq_s", [T, E], BF16).ap()
    q_s = nc.dram_tensor("q_s", [E, TH], BF16).ap()
    k_s = nc.dram_tensor("k_s", [E, TH], BF16).ap()
    vt_s = nc.dram_tensor("vt_s", [TH, E], BF16).ap()
    att_s = nc.dram_tensor("att_s", [E, T], BF16).ap()
    inter_s = nc.dram_tensor("inter_s", [II, T], BF16).ap()
    if dbg:
        dbg_yq = nc.dram_tensor("dbg_yq", [128, II * (T // max(1, T // 1024)) // 128], F32).ap()
        dbg_h = nc.dram_tensor("dbg_h", [128, II * (T // max(1, T // 1024)) // 128], F32).ap()
        dbg_m = nc.dram_tensor("dbg_m", [128, 4 * (T // max(1, T // 1024))], F32).ap()

    h128_d = nc.inline_tensor(_had(128).astype(ml_dtypes.bfloat16), "h128c")
    h64 = _had(64)
    bdm = np.zeros((128, 128), np.float32)
    bdm[:64, :64] = h64
    bdm[64:, 64:] = h64
    bd64_d = nc.inline_tensor(bdm.astype(ml_dtypes.bfloat16), "bd64c")
    idnb_d = nc.inline_tensor(np.eye(128, dtype=ml_dtypes.bfloat16), "idnbc")
    idnf_d = nc.inline_tensor(np.eye(128, dtype=np.float32), "idnfc")
    ones_d = nc.inline_tensor(np.ones((1, 128), np.float32), "ones1c")
    pp = np.arange(128)[:, None]
    jl = np.arange(128)[None, :]
    mb = np.zeros((128, 256), np.float32)
    mb[:, 0:128] = np.where(jl >= pp, 0.0, NEGM)
    mb[:, 128:256] = np.where(jl <= pp, 0.0, NEGM)
    maskb_d = nc.inline_tensor(mb, "maskbc")
    cst_d = nc.inline_tensor(
        np.tile(np.array([[MAGIC, EPS]], np.float32), (128, 1)), "cstc")

    P = Prog(nc)
    P.mksem("te", "ve", "ae")
    P.mkdma(nc.sync, "cs", "xs", "ws", "wt", "wu", "so", "sp")
    P.mkdma(nc.scalar, "la", "lb", "ka", "kb", "xa", "xb", "wa", "wb")
    P.mkdma(nc.gpsimd, "wc", "sa", "sb")
    sb = P.enter

    idnb = sb(nc.sbuf_tensor("idnb", [128, 128], BF16))
    idnf = sb(nc.sbuf_tensor("idnf", [128, 128], F32))
    h128 = sb(nc.sbuf_tensor("h128", [128, 128], BF16))
    bd64 = sb(nc.sbuf_tensor("bd64", [128, 128], BF16))
    ones1 = sb(nc.sbuf_tensor("ones1", [1, 128], F32))
    maskb_t = sb(nc.sbuf_tensor("maskb_t", [128, 256], F32))
    maskz_t = sb(nc.sbuf_tensor("maskz_t", [128, 4 * 640], F32))
    acc = sb(nc.sbuf_tensor("acc", [128, 3], F32))
    mbar = sb(nc.sbuf_tensor("mbar", [128, 3], F32))
    swi = sb(nc.sbuf_tensor("swi", [128, 3], F32))
    cst = sb(nc.sbuf_tensor("cst", [128, 2], F32))
    nc.const_aps.register(F32, MAGIC, cst[:, 0:1])
    nc.const_aps.register(F32, EPS, cst[:, 1:2])

    P.dma(nc.sync, idnb[:], idnb_d.ap(), "cs")
    P.dma(nc.sync, idnf[:], idnf_d.ap(), "cs")
    P.dma(nc.sync, h128[:], h128_d.ap(), "cs")
    P.dma(nc.sync, bd64[:], bd64_d.ap(), "cs")
    P.dma(nc.sync, ones1[:], ones_d.ap(), "cs")
    P.dma(nc.sync, maskb_t[:], maskb_d.ap(), "cs")
    P.dma(nc.sync, cst[:], cst_d.ap(), "cs")
    for s in range(4):
        P.dma(nc.sync, maskz_t[:, s * 640:(s + 1) * 640], maskz[s], "cs")
    P.wdeps(nc.vector, [P.mark("cs")])
    nc.vector.memset(acc[:], 0.0)
    P.tick(nc.vector.drain(), "ve")
    P.barrier()

    for i in range(6):
        P.dma(nc.gpsimd, wqkv_bf[i * 1024:(i + 1) * 1024, :],
              w_qkv[i * 1024:(i + 1) * 1024, :], "wc")
    for i in range(2):
        P.dma(nc.gpsimd, wout_bf[i * 1024:(i + 1) * 1024, :],
              w_out[i * 1024:(i + 1) * 1024, :], "wc")
    castdone = P.mark("wc")

    from contextlib import ExitStack
    wst = ExitStack()
    w1b = wst.enter_context(nc.sbuf_tensor("w1b", [128, 2 * 2048], F32))
    w1t = wst.enter_context(nc.sbuf_tensor("w1t", [128, 1], F32))
    w2t = wst.enter_context(nc.sbuf_tensor("w2t", [128, 2048], F32))
    w2u = wst.enter_context(nc.sbuf_tensor("w2u", [128, 2048], F32))
    w2o = wst.enter_context(nc.sbuf_tensor("w2o", [128, 2 * 2048], BF16))

    wtiles = []
    for r in range(II // 128):
        wtiles.append((0, w_gate[r * 128:(r + 1) * 128, :]))
    for r in range(II // 128):
        wtiles.append((1, w_up[r * 128:(r + 1) * 128, :]))
    for r in range(E // 128):
        for cb in range(4):
            wtiles.append((2, w_down[r * 128:(r + 1) * 128,
                                     cb * 2048:(cb + 1) * 2048]))
    ws = {"free": [(), ()], "i": 0, "rddone": (), "findone": (),
          "tfree": (), "ofree": [(), ()]}

    def wjob_mean():
        i = ws["i"]; ws["i"] += 1
        wi, srcw = wtiles[i]
        b = i % 2
        buf = w1b[:, b * 2048:(b + 1) * 2048]
        ld = P.dma(nc.scalar, buf, srcw, "wa" if b == 0 else "wb",
                   deps=[ws["free"][b]])
        P.wdeps(nc.vector, [ld])
        nc.vector.reduce_sum(w1t[:], buf, axis=AX.X, apply_absolute_value=True)
        ws["free"][b] = P.tick(nc.vector.drain(), "ve")
        nc.vector.tensor_add(acc[:, wi:wi + 1], acc[:, wi:wi + 1], w1t[:])
        nc.vector.drain()

    def wjob_finalize():
        psw1 = ws["psw1"]
        rddone = P.tick(nc.vector.engine_nop(), "ve")
        for wi in range(3):
            P.wdeps(nc.tensor, [rddone])
            tr = P.tick(nc.tensor.transpose(psw1[0:1, wi * 128:(wi + 1) * 128],
                                            acc[:, wi:wi + 1], idnf[:]), "te")
            P.wdeps(nc.vector, [tr])
            nc.vector.reduce_sum(w1t[0:1, 0:1], psw1[0:1, wi * 128:(wi + 1) * 128],
                                 axis=AX.X)
            nc.vector.drain()
            nc.vector.tensor_scalar(w1t[0:1, 0:1], w1t[0:1, 0:1],
                                    1.0 / float(II * E), EPS, AL.mult, AL.max)
            sc = P.tick(nc.vector.drain(), "ve")
            P.wdeps(nc.tensor, [sc])
            mm = P.tick(nc.tensor.matmul(psw1[0:128, 384 + wi:385 + wi], ones1[:],
                                         w1t[0:1, 0:1], start=True, stop=True), "te")
            P.wdeps(nc.vector, [mm])
            nc.vector.tensor_copy(mbar[:, wi:wi + 1], psw1[:, 384 + wi:385 + wi])
            nc.vector.drain()
            nc.vector.reciprocal(swi[:, wi:wi + 1], mbar[:, wi:wi + 1])
            rddone = P.tick(nc.vector.drain(), "ve")
        ws["findone"] = rddone
        ws["i"] = 0

    def wjob_tern():
        i = ws["i"]; ws["i"] += 1
        wi, srcw = wtiles[i]
        b = i % 2
        buf = w1b[:, b * 2048:(b + 1) * 2048]
        ld = P.dma(nc.scalar, buf, srcw, "wa" if b == 0 else "wb",
                   deps=[ws["free"][b]])
        P.wdeps(nc.vector, [ld, ws["findone"], ws["tfree"]])
        nc.vector.tensor_scalar(w2t[:], buf, swi[:, wi:wi + 1], 0.5,
                                AL.mult, AL.is_gt)
        nc.vector.drain()
        nc.vector.tensor_scalar(w2u[:], buf, swi[:, wi:wi + 1], -0.5,
                                AL.mult, AL.is_lt)
        ws["free"][b] = P.tick(nc.vector.drain(), "ve")
        P.wdeps(nc.vector, [ws["ofree"][b]])
        wi2, r, cb = wdst[i]
        ob = w2o[:, b * 2048:(b + 1) * 2048]
        if wi2 == 0:
            dst = tern_g[r * 128:(r + 1) * 128, :]
            nc.vector.tensor_tensor(ob, w2t[:], w2u[:], AL.subtract)
        elif wi2 == 1:
            dst = tern_u[r * 128:(r + 1) * 128, :]
            nc.vector.tensor_tensor(ob, w2t[:], w2u[:], AL.subtract)
        else:
            obp = bass.AP(w2o, b * 2048, [[2 * 2048, 128], [32, 64], [16, 2], [1, 16]])
            t1p = bass.AP(w2t, 0, [[2048, 128], [1, 64], [64, 2], [128, 16]])
            t2p = bass.AP(w2u, 0, [[2048, 128], [1, 64], [64, 2], [128, 16]])
            nc.vector.tensor_tensor(obp, t1p, t2p, AL.subtract)
            ws["tfree"] = td = P.tick(nc.vector.drain(), "ve")
            sem2 = "sa" if b == 0 else "sb"
            for dlt in range(2):
                dst = bass.AP(wds.tensor, r * 128 * II + cb * 16 + dlt * 64,
                              [[II, 128], [128, 64], [1, 16]])
                obh = bass.AP(w2o, b * 2048 + dlt * 16,
                              [[2 * 2048, 128], [32, 64], [1, 16]])
                P.dma(nc.gpsimd, dst, obh, sem2, deps=[td] if dlt == 0 else ())
            ws["ofree"][b] = P.mark(sem2)
            return
        ws["tfree"] = td = P.tick(nc.vector.drain(), "ve")
        ws["ofree"][b] = P.dma(nc.gpsimd, dst, ob, "sa" if b == 0 else "sb",
                               deps=[td])

    NW = len(wtiles)
    wdst = {}
    for k, (wi, _) in enumerate(wtiles):
        if wi == 0:
            wdst[k] = (0, k, 0)
        elif wi == 1:
            wdst[k] = (1, k - II // 128, 0)
        else:
            kk = k - 2 * (II // 128)
            wdst[k] = (2, kk // 4, kk % 4)

    wjobs = ([wjob_mean] * NW) + [wjob_finalize] + [wjob_tern] * NW

    def wpump(n=1):
        for _ in range(n):
            if wjobs:
                wjobs.pop(0)()

    NT = TH // 128
    with (
        nc.sbuf_tensor("xb1", [128, 2 * 2048], F32) as xb1,
        nc.sbuf_tensor("xnf", [128, 2048], F32) as xnf,
        nc.sbuf_tensor("qt1", [128, 2048], F32) as qt1,
        nc.sbuf_tensor("qi1", [128, 2048], mybir.dt.int32) as qi1,
        nc.sbuf_tensor("qd1", [128, 2048], F32) as qd1,
        nc.sbuf_tensor("qc1", [128, 2048], BF16) as qc1,
        nc.sbuf_tensor("qc2", [128, 2048], BF16) as qc2,
        nc.sbuf_tensor("xnbb", [128, 2 * 2048], BF16) as xnbb,
        nc.sbuf_tensor("xqb", [128, 2 * 2048], BF16) as xqb,
        nc.sbuf_tensor("ln", [128, 40], F32) as ln,
    ):
        xfree = [(), ()]
        bfree = [(), ()]
        qfree = [(), ()]
        prev_rs = ()
        for k in range(NT):
            b = k % 2
            xt = xb1[:, b * 2048:(b + 1) * 2048]
            ld = P.dma(nc.scalar, xt, xc[k * 128:(k + 1) * 128, :],
                       "la" if b == 0 else "lb", deps=[xfree[b]])
            P.wdeps(nc.vector, [ld])
            if prev_rs:
                P.wdeps(nc.vector, [prev_rs])
            for q in range(4):
                nc.vector.bn_stats(ln[:, 16 + 6 * q:16 + 6 * (q + 1)],
                                   xt[:, q * 512:(q + 1) * 512])
                nc.vector.drain()
            nc.vector.bn_aggr(ln[:, 6:8], ln[:, 16:40])
            st = P.tick(nc.vector.drain(), "ve")
            P.wdeps(nc.scalar, [st])
            nc.scalar.drain()
            nc.scalar.activation(ln[:, 8:9], ln[:, 7:8], AF.Ln, bias=EPS, scale=1.0)
            nc.scalar.drain()
            rs = prev_rs = P.tick(nc.scalar.activation(ln[:, 9:10], ln[:, 8:9], AF.Exp,
                                                       bias=0.0, scale=-0.5), "ae")
            P.wdeps(nc.vector, [rs, bfree[b]] if bfree[b] else [rs])
            nc.vector.tensor_scalar(xnbb[:, b * 2048:(b + 1) * 2048], xt,
                                    ln[:, 6:7], ln[:, 9:10], AL.subtract, AL.mult)
            bdone = P.tick(nc.vector.drain(), "ve")
            nc.vector.tensor_scalar(xnf[:], xt, ln[:, 6:7], ln[:, 9:10],
                                    AL.subtract, AL.mult)
            xfree[b] = P.tick(nc.vector.drain(), "ve")
            P.dma(nc.gpsimd, xnb_s[k * 128:(k + 1) * 128, :],
                  xnbb[:, b * 2048:(b + 1) * 2048],
                  "sa" if b == 0 else "sb", deps=[bdone])
            bfree[b] = P.mark("sa" if b == 0 else "sb")
            if k >= HALO // 128:
                nc.vector.reduce_max(ln[:, 10:11], xnf[:], axis=AX.X,
                                     apply_absolute_value=True)
                nc.vector.drain()
                nc.vector.tensor_scalar(ln[:, 11:12], ln[:, 10:11], EPS, 1.0 / 128.0,
                                        AL.max, AL.mult)
                nc.vector.drain()
                nc.vector.tensor_scalar(ln[:, 12:13], ln[:, 10:11], EPS, None, AL.max)
                nc.vector.drain()
                nc.vector.reciprocal(ln[:, 13:14], ln[:, 12:13])
                nc.vector.drain()
                nc.vector.tensor_scalar(ln[:, 14:15], ln[:, 13:14], 128.0, None,
                                        AL.mult)
                nc.vector.drain()
                nc.vector.tensor_scalar(qt1[:], xnf[:], ln[:, 14:15], None, AL.mult)
                nc.vector.drain()
                nc.vector.tensor_scalar(qi1[:], qt1[:], 1024.0, None, AL.add)
                nc.vector.drain()
                nc.vector.scalar_tensor_tensor(qd1[:], qt1[:], 1024.0, qi1[:],
                                               AL.add, AL.subtract)
                nc.vector.drain()
                nc.vector.tensor_scalar(qc1[:], qd1[:], 0.5, None, AL.is_gt)
                nc.vector.drain()
                nc.vector.tensor_scalar(qc2[:], qd1[:], 1.5, None, AL.is_gt)
                nc.vector.drain()
                nc.vector.tensor_scalar(qd1[:], qd1[:], -0.5, None, AL.is_lt)
                nc.vector.drain()
                nc.vector.tensor_sub(qc2[:], qc2[:], qd1[:])
                nc.vector.drain()
                nc.vector.scalar_tensor_tensor(qt1[:], qi1[:], -1024.0, qc1[:],
                                               AL.add, AL.add)
                nc.vector.drain()
                nc.vector.tensor_add(qt1[:], qt1[:], qc2[:])
                nc.vector.drain()
                nc.vector.tensor_scalar(qt1[:], qt1[:], -128.0, 127.0, AL.max, AL.min)
                nc.vector.drain()
                P.wdeps(nc.vector, [qfree[b]])
                nc.vector.tensor_scalar(xqb[:, b * 2048:(b + 1) * 2048], qt1[:],
                                        ln[:, 11:12], None, AL.mult)
                qdone = P.tick(nc.vector.drain(), "ve")
                kq = k - HALO // 128
                P.dma(nc.gpsimd, xnq_s[kq * 128:(kq + 1) * 128, :],
                      xqb[:, b * 2048:(b + 1) * 2048],
                      "sa" if b == 0 else "sb", deps=[qdone])
                qfree[b] = bfree[b] = P.mark("sa" if b == 0 else "sb")
            wpump(2)
    P.barrier()

    sweeps = []
    t0 = 0
    while t0 < TH:
        ts = min(1024, TH - t0)
        sweeps.append((t0, ts))
        t0 += ts
    with (
        nc.sbuf_tensor("xbs", [128, NE * 1024], BF16) as xbs,
        nc.sbuf_tensor("wsl", [128, NE * 1024], BF16) as wsl,
        nc.sbuf_tensor("pje", [128, 2 * 1024], BF16) as pje,
        nc.psum_tensor("psp", [128, 4 * 512], F32) as psp,
        nc.psum_tensor("psw1", [128, 512], F32) as psw1,
    ):
        ws["psw1"] = psw1
        xbs_free = ()
        wsl_free = ()
        pje_free = [(), ()]
        ps_free = [(), ()]
        for (st, ts) in sweeps:
            nsl = ts // 512
            for f in range(NE):
                P.dmat(nc.sync, xbs[:, f * 1024:f * 1024 + ts],
                       xnb_s[st:st + ts, f * 128:(f + 1) * 128], "xs",
                       deps=(xbs_free,) if f == 0 else ())
            xload = P.mark("xs")
            for vh in range(2):
                for f in range(NE):
                    P.dmat(nc.sync, wsl[:, f * 1024:(f + 1) * 1024],
                           wqkv_bf[4096 + vh * 1024:4096 + (vh + 1) * 1024,
                                   f * 128:(f + 1) * 128], "ws",
                           deps=(wsl_free, castdone) if f == 0 else ())
                wload = P.mark("ws")
                P.wdeps(nc.tensor, [xload, wload])
                for tb in range(ts // 128):
                    pb = 2 * (tb % 2)
                    P.wdeps(nc.tensor, [ps_free[tb % 2]] if ps_free[tb % 2] else [])
                    mi = None
                    for f in range(NE):
                        for es in range(2):
                            mi = nc.tensor.matmul(
                                psp[:, (pb + es) * 512:(pb + es + 1) * 512],
                                xbs[:, f * 1024 + tb * 128:f * 1024 + (tb + 1) * 128],
                                wsl[:, f * 1024 + es * 512:f * 1024 + (es + 1) * 512],
                                start=(f == 0), stop=(f == NE - 1))
                    mm = P.tick(mi, "te")
                    P.wdeps(nc.scalar, [mm, pje_free[tb % 2]] if pje_free[tb % 2] else [mm])
                    nc.scalar.drain()
                    ev = P.tick(nc.scalar.activation(
                        pje[:, (tb % 2) * 1024:(tb % 2 + 1) * 1024],
                        psp[:, pb * 512:(pb + 2) * 512], AF.Copy), "ae")
                    ps_free[tb % 2] = ev
                    pje_free[tb % 2] = P.dma(
                        nc.sync, vt_s[st + tb * 128:st + (tb + 1) * 128,
                                      vh * 1024:(vh + 1) * 1024],
                        pje[:, (tb % 2) * 1024:(tb % 2 + 1) * 1024],
                        "so" if tb % 2 == 0 else "sp", deps=[ev])
                    wpump(2)
                wsl_free = P.mark("te")
            for og in range(4):
                for f in range(NE):
                    P.dmat(nc.sync, wsl[:, f * 1024:(f + 1) * 1024],
                           wqkv_bf[og * 1024:(og + 1) * 1024, f * 128:(f + 1) * 128],
                           "ws", deps=(wsl_free, castdone) if f == 0 else ())
                wload = P.mark("ws")
                P.wdeps(nc.tensor, [wload])
                for ot in range(8):
                    pb = nsl * (ot % 2)
                    P.wdeps(nc.tensor, [ps_free[ot % 2]] if ps_free[ot % 2] else [])
                    mi = None
                    for f in range(NE):
                        for sl in range(nsl):
                            mi = nc.tensor.matmul(
                                psp[:, (pb + sl) * 512:(pb + sl + 1) * 512],
                                wsl[:, f * 1024 + ot * 128:f * 1024 + (ot + 1) * 128],
                                xbs[:, f * 1024 + sl * 512:f * 1024 + (sl + 1) * 512],
                                start=(f == 0), stop=(f == NE - 1))
                    mm = P.tick(mi, "te")
                    P.wdeps(nc.scalar, [mm, pje_free[ot % 2]] if pje_free[ot % 2] else [mm])
                    nc.scalar.drain()
                    ev = P.tick(nc.scalar.activation(
                        pje[:, (ot % 2) * 1024:(ot % 2) * 1024 + ts],
                        psp[:, pb * 512:pb * 512 + ts], AF.Copy), "ae")
                    ps_free[ot % 2] = ev
                    orow = og * 1024 + ot * 128
                    dst = q_s if orow < 2048 else k_s
                    orow2 = orow % 2048
                    pje_free[ot % 2] = P.dma(
                        nc.sync, dst[orow2:orow2 + 128, st:st + ts],
                        pje[:, (ot % 2) * 1024:(ot % 2) * 1024 + ts],
                        "so" if ot % 2 == 0 else "sp", deps=[ev])
                    wpump(2)
                wsl_free = P.mark("te")
            xbs_free = P.mark("te")
    P.barrier()

    with (
        nc.sbuf_tensor("Kw", [128, NE * 1024], BF16) as Kw,
        nc.sbuf_tensor("VTw", [128, 8 * 2048], BF16) as VTw,
        nc.sbuf_tensor("Qc", [128, NE * 512], BF16) as Qc,
        nc.sbuf_tensor("Pe", [128, 4 * 640], BF16) as Pe,
        nc.sbuf_tensor("PT", [128, 8 * 512], BF16) as PT,
        nc.sbuf_tensor("Dt", [128, 256], BF16) as Dt,
        nc.sbuf_tensor("aveb", [128, 2 * 512], BF16) as aveb,
        nc.sbuf_tensor("sm", [128, 8], F32) as sm,
        nc.psum_tensor("psS", [128, 2 * 1024], F32) as psS,
        nc.psum_tensor("psPT", [128, 2 * 512], F32) as psPT,
        nc.psum_tensor("psAV", [128, 2 * 512], F32) as psAV,
    ):
        ldfree = ()
        s_free = [(), ()]
        pt_free = [(), ()]
        av_free = [(), ()]
        ab_free = [(), ()]
        pe_free = ()
        dt_free = [(), ()]
        for n in range(NCH):
            kw = n * 512
            qg = HALO + n * 512
            P.wdeps(nc.scalar, [ldfree])
            for hb in range(NE):
                P.dma(nc.scalar, Kw[:, hb * 1024:(hb + 1) * 1024],
                      k_s[hb * 128:(hb + 1) * 128, kw:kw + 1024], "ka")
            for kb in range(8):
                P.dma(nc.scalar, VTw[:, kb * 2048:(kb + 1) * 2048],
                      vt_s[kw + kb * 128:kw + (kb + 1) * 128, :], "ka")
            for hb in range(NE):
                P.dma(nc.scalar, Qc[:, hb * 512:(hb + 1) * 512],
                      q_s[hb * 128:(hb + 1) * 128, qg:qg + 512], "ka")
            ld = P.mark("ka")
            P.wdeps(nc.tensor, [ld])
            P.wdeps(nc.vector, [ld])

            def p2b_scores(j):
                h, s = j // 4, j % 4
                pp = j % 2
                ps = psS[:, pp * 1024:pp * 1024 + 640]
                P.wdeps(nc.tensor, [s_free[pp]])
                nc.tensor.matmul(ps[:, 0:512],
                                 Qc[:, h * 512 + s * 128:h * 512 + (s + 1) * 128],
                                 Kw[:, h * 1024 + s * 128:h * 1024 + s * 128 + 512],
                                 start=True, stop=True)
                return P.tick(nc.tensor.matmul(
                    ps[:, 512:640],
                    Qc[:, h * 512 + s * 128:h * 512 + (s + 1) * 128],
                    Kw[:, h * 1024 + s * 128 + 512:h * 1024 + s * 128 + 640],
                    start=True, stop=True), "te")

            def p2b_softmax(j, sc2):
                h, s = j // 4, j % 4
                pp = j % 2
                ps = psS[:, pp * 1024:pp * 1024 + 640]
                P.wdeps(nc.vector, [sc2])
                if n == 0:
                    nc.vector.tensor_add(ps, ps, maskz_t[:, s * 640:(s + 1) * 640])
                else:
                    nc.vector.tensor_add(ps[:, 0:128], ps[:, 0:128],
                                         maskb_t[:, 0:128])
                    nc.vector.drain()
                    nc.vector.tensor_add(ps[:, 512:640], ps[:, 512:640],
                                         maskb_t[:, 128:256])
                nc.vector.drain()
                nc.vector.reduce_max(sm[:, pp * 4:pp * 4 + 1], ps, axis=AX.X)
                nc.vector.drain()
                nc.vector.tensor_scalar(sm[:, pp * 4 + 1:pp * 4 + 2],
                                        sm[:, pp * 4:pp * 4 + 1], -SM_SCALE, None,
                                        AL.mult)
                mx = P.tick(nc.vector.drain(), "ve")
                P.wdeps(nc.scalar, [mx, pe_free])
                nc.scalar.drain()
                s_free[pp] = ex = P.tick(nc.scalar.activation(
                    Pe[:, s * 640:(s + 1) * 640], ps, AF.Exp,
                    bias=sm[:, pp * 4 + 1:pp * 4 + 2], scale=SM_SCALE,
                    accum_out=sm[:, pp * 4 + 2:pp * 4 + 3]), "ae")
                P.wdeps(nc.vector, [ex, dt_free[pp]])
                nc.vector.reciprocal(sm[:, pp * 4 + 3:pp * 4 + 4],
                                     sm[:, pp * 4 + 2:pp * 4 + 3])
                nc.vector.drain()
                nc.vector.tensor_scalar(Dt[:, pp * 128:(pp + 1) * 128], idnb[:],
                                        sm[:, pp * 4 + 3:pp * 4 + 4], None, AL.mult)
                return P.tick(nc.vector.drain(), "ve")

            def p2b_transposes(j, dv):
                h, s = j // 4, j % 4
                pp = j % 2
                P.wdeps(nc.tensor, [dv])
                for jb in range(5):
                    c = s + jb
                    P.wdeps(nc.tensor, [pt_free[jb % 2]])
                    tr = P.tick(nc.tensor.matmul(
                        psPT[:, (jb % 2) * 512:(jb % 2) * 512 + 128],
                        Pe[:, s * 640 + jb * 128:s * 640 + (jb + 1) * 128],
                        Dt[:, pp * 128:(pp + 1) * 128], start=True, stop=True), "te")
                    if jb == 4:
                        dt_free[pp] = tr
                    P.wdeps(nc.scalar, [tr])
                    nc.scalar.drain()
                    pt_free[jb % 2] = P.tick(nc.scalar.activation(
                        PT[:, c * 512 + s * 128:c * 512 + (s + 1) * 128],
                        psPT[:, (jb % 2) * 512:(jb % 2) * 512 + 128],
                        AF.Copy), "ae")

            def p2b_av(h):
                nonlocal pe_free
                hp = h % 2
                pa = psAV[:, hp * 512:(hp + 1) * 512]
                P.wdeps(nc.tensor, [av_free[hp], P.mark("ve"), P.mark("ae")])
                mi = None
                corder = [4] + [c for c in range(8) if c != 4]
                for ci_, c in enumerate(corder):
                    a0 = max(0, c - 4) * 128
                    a1 = (min(3, c) + 1) * 128
                    mi = nc.tensor.matmul(pa[:, a0:a1],
                                          VTw[:, c * 2048 + h * 128:c * 2048 + (h + 1) * 128],
                                          PT[:, c * 512 + a0:c * 512 + a1],
                                          start=(ci_ == 0), stop=(ci_ == 7))
                pe_free = av = P.tick(mi, "te")
                P.wdeps(nc.scalar, [av, ab_free[hp]])
                nc.scalar.drain()
                av_free[hp] = ev = P.tick(nc.scalar.activation(
                    aveb[:, hp * 512:(hp + 1) * 512], pa, AF.Copy), "ae")
                ab_free[hp] = P.dma(nc.sync,
                                    att_s[h * 128:(h + 1) * 128, n * 512:(n + 1) * 512],
                                    aveb[:, hp * 512:(hp + 1) * 512],
                                    "so" if hp == 0 else "sp", deps=[ev])

            prev = None
            for j in range(NH * 4 + 1):
                if j < NH * 4:
                    sc2 = p2b_scores(j)
                if prev is not None:
                    jp, sc2p = prev
                    dv = p2b_softmax(jp, sc2p)
                    p2b_transposes(jp, dv)
                    if jp % 4 == 3:
                        p2b_av(jp // 4)
                        wpump(1)
                prev = (j, sc2) if j < NH * 4 else None
            ldfree = P.mark("te")
        while wjobs:
            wpump(1)
    wst.close()
    P.barrier()

    with (
        nc.sbuf_tensor("xqs", [128, NE * 2048], BF16) as xqs,
        nc.sbuf_tensor("tsl", [128, 2 * NE * 1024], BF16) as tsl,
        nc.sbuf_tensor("sg", [128, 8 * 2048], BF16) as sg,
        nc.sbuf_tensor("ie", [128, 2 * 2048], BF16) as ie,
        nc.psum_tensor("psf", [128, 8 * 512], F32) as psf,
    ):
        TSL = max(1, T // 512)
        for f in range(NE):
            P.dmat(nc.sync, xqs[:, f * 2048:f * 2048 + T],
                   xnq_s[0:T, f * 128:(f + 1) * 128], "xs")
        xqload = P.mark("xs")
        passes = []
        for og in range(II // 1024):
            passes.append(("g", og))
            passes.append(("u", og))
        slab_ready = {}
        slab_free = [(), ()]

        def load_slab(i):
            w, og = passes[i]
            b = i % 2
            src = tern_g if w == "g" else tern_u
            sname = "ws" if b == 0 else "wt"
            for f in range(NE):
                P.dmat(nc.sync, tsl[:, b * NE * 1024 + f * 1024:b * NE * 1024 + (f + 1) * 1024],
                       src[og * 1024:(og + 1) * 1024, f * 128:(f + 1) * 128], sname,
                       deps=(slab_free[b],) if f == 0 else ())
            slab_ready[i] = P.mark(sname)

        load_slab(0)
        load_slab(1)
        ps_free = [(), ()]
        sg_done = [()] * 8
        sg_free = [()] * 8
        ie_free = [(), ()]
        for i, (w, og) in enumerate(passes):
            b = i % 2
            P.wdeps(nc.tensor, [slab_ready[i], xqload])
            for ot in range(8):
                pb = (ot % 2) * 4
                P.wdeps(nc.tensor, [ps_free[ot % 2]] if ps_free[ot % 2] else [])
                mi = None
                for f in range(NE):
                    for sl in range(TSL):
                        mi = nc.tensor.matmul(
                            psf[:, (pb + sl) * 512:(pb + sl + 1) * 512],
                            tsl[:, b * NE * 1024 + f * 1024 + ot * 128:b * NE * 1024 + f * 1024 + (ot + 1) * 128],
                            xqs[:, f * 2048 + sl * 512:f * 2048 + (sl + 1) * 512],
                            start=(f == 0), stop=(f == NE - 1))
                mm = P.tick(mi, "te")
                if w == "g":
                    P.wdeps(nc.scalar, [mm])
                    if sg_free[ot]:
                        P.wdeps(nc.scalar, [sg_free[ot]])
                    nc.scalar.drain()
                    ps_free[ot % 2] = sg_done[ot] = P.tick(
                        nc.scalar.activation(sg[:, ot * 2048:ot * 2048 + T],
                                             psf[:, pb * 512:pb * 512 + T],
                                             AF.Silu, bias=0.0,
                                             scale=mbar[:, 0:1]), "ae")
                else:
                    P.wdeps(nc.vector, [mm, sg_done[ot]])
                    if ie_free[ot % 2]:
                        P.wdeps(nc.vector, [ie_free[ot % 2]])
                    nc.vector.scalar_tensor_tensor(
                        ie[:, (ot % 2) * 2048:(ot % 2) * 2048 + T],
                        psf[:, pb * 512:pb * 512 + T],
                        mbar[:, 1:2], sg[:, ot * 2048:ot * 2048 + T],
                        AL.mult, AL.mult)
                    ps_free[ot % 2] = iv = sg_free[ot] = P.tick(nc.vector.drain(), "ve")
                    ie_free[ot % 2] = P.dma(
                        nc.gpsimd,
                        inter_s[og * 1024 + ot * 128:og * 1024 + (ot + 1) * 128, 0:T],
                        ie[:, (ot % 2) * 2048:(ot % 2) * 2048 + T],
                        "sa" if ot % 2 == 0 else "sb", deps=[iv])
            slab_free[b] = P.mark("te")
            if i + 2 < len(passes):
                load_slab(i + 2)
    P.barrier()

    NTILE = max(1, T // 1024)
    TT = T // NTILE
    NTG = TT // 128
    with (
        nc.sbuf_tensor("bufA", [128, NI * TT], BF16) as bufA,
        nc.sbuf_tensor("zz", [128, 2 * 1024], BF16) as zz,
        nc.sbuf_tensor("Mx", [128, TT], F32) as Mx,
        nc.sbuf_tensor("syc", [128, 4 * 16], F32) as syc,
        nc.sbuf_tensor("syr", [1, 2 * TT], F32) as syr,
        nc.sbuf_tensor("Sb", [128, 2 * TT], F32) as Sb,
        nc.sbuf_tensor("qtm", [128, TT], F32) as qtm,
        nc.sbuf_tensor("qi2", [128, TT], mybir.dt.int32) as qi2,
        nc.sbuf_tensor("qd2", [128, TT], F32) as qd2,
        nc.sbuf_tensor("qc3", [128, TT], BF16) as qc3,
        nc.sbuf_tensor("qc4", [128, TT], BF16) as qc4,
        nc.sbuf_tensor("wdsl", [128, 3 * 1024], BF16) as wdsl,
        nc.sbuf_tensor("atl", [128, 8 * 512], BF16) as atl,
        nc.sbuf_tensor("xr", [128, 2 * 512], F32) as xr,
        nc.sbuf_tensor("fo", [128, 2 * 512], F32) as fo,
    ):
        buf_free = ()
        atl_free = ()
        wfree = [(), (), ()]
        x_free = [(), ()]
        f_free = [(), ()]
        for tt in range(NTILE):
            tof = tt * TT
            P.wdeps(nc.scalar, [buf_free])
            for ci in range(NI):
                P.dma(nc.scalar, bufA[:, ci * TT:(ci + 1) * TT],
                      inter_s[ci * 128:(ci + 1) * 128, tof:tof + TT], "ka")
            ild = P.mark("ka")
            P.wdeps(nc.tensor, [ild])
            with nc.psum_tensor(f"psZ{tt}", [128, 2 * 512], F32) as psZ:
                z_free = [(), ()]
                for ci in range(NI):
                    for th in range(TT // 512):
                        j = ci * (TT // 512) + th
                        pz = j % 2
                        P.wdeps(nc.tensor, [z_free[pz]] if z_free[pz] else [])
                        mm = P.tick(nc.tensor.matmul(
                            psZ[:, pz * 512:(pz + 1) * 512], h128[:],
                            bufA[:, ci * TT + th * 512:ci * TT + (th + 1) * 512],
                            start=True, stop=True), "te")
                        P.wdeps(nc.scalar, [mm])
                        nc.scalar.drain()
                        z_free[pz] = P.tick(nc.scalar.activation(
                            bufA[:, ci * TT + th * 512:ci * TT + (th + 1) * 512],
                            psZ[:, pz * 512:(pz + 1) * 512], AF.Copy), "ae")
                s1done = P.mark("ae")
            with (
                nc.psum_tensor(f"psT{tt}", [128, 2 * 1024], BF16) as psT,
                nc.psum_tensor(f"psY{tt}", [128, 2 * 1024], F32) as psY,
            ):
                P.wdeps(nc.tensor, [s1done])
                t_free = [(), ()]
                y_free = [(), ()]
                zz_free = [(), ()]
                for g in range(TT // 16):
                    gp = g % 2
                    P.wdeps(nc.tensor, [t_free[gp]])
                    mi = None
                    for tl in range(16):
                        t = g * 16 + tl
                        for d in range(2):
                            mi = nc.tensor.transpose(
                                psT[d * 64:(d + 1) * 64,
                                    gp * 1024 + tl * 64:gp * 1024 + (tl + 1) * 64],
                                bufA[d * 64:(d + 1) * 64, t::TT],
                                idnb[d * 64:(d + 1) * 64, d * 64:(d + 1) * 64])
                    tr = P.tick(mi, "te")
                    P.wdeps(nc.vector, [tr, zz_free[gp]])
                    nc.vector.tensor_copy(zz[:, gp * 1024:(gp + 1) * 1024],
                                          psT[:, gp * 1024:(gp + 1) * 1024])
                    t_free[gp] = zc = P.tick(nc.vector.drain(), "ve")
                    P.wdeps(nc.tensor, [zc, y_free[gp]])
                    nc.tensor.matmul(psY[:, gp * 1024:gp * 1024 + 512], bd64[:],
                                     zz[:, gp * 1024:gp * 1024 + 512],
                                     start=True, stop=True)
                    s2 = P.tick(nc.tensor.matmul(psY[:, gp * 1024 + 512:(gp + 1) * 1024],
                                                 bd64[:],
                                                 zz[:, gp * 1024 + 512:(gp + 1) * 1024],
                                                 start=True, stop=True), "te")
                    zz_free[gp] = s2
                    P.wdeps(nc.vector, [s2])
                    nc.vector.reduce_max(
                        Mx[:, g * 16:(g + 1) * 16],
                        psY[:, gp * 1024:(gp + 1) * 1024].rearrange("p (a b) -> p a b", a=16),
                        axis=AX.X, apply_absolute_value=True)
                    m8 = P.tick(nc.vector.drain(), "ve")
                    P.wdeps(nc.scalar, [m8])
                    nc.scalar.drain()
                    dst = bass.AP(bufA, g * 16, [[NI * TT, 128], [1, 16], [TT, 64]])
                    src2 = psY[:, gp * 1024:(gp + 1) * 1024].rearrange("p (a b) -> p a b", a=16)
                    y_free[gp] = P.tick(nc.scalar.activation(dst, src2, AF.Copy), "ae")
                hdone_ae = P.mark("ae")
                hdone_ve = P.mark("ve")
            with nc.psum_tensor(f"psA{tt}", [128, 2 * 512], F32) as psA:
                P.wdeps(nc.tensor, [hdone_ae, hdone_ve])
                a_free = [(), ()]
                for tg in range(NTG):
                    ap = tg % 2
                    P.wdeps(nc.tensor, [a_free[ap]] if a_free[ap] else [])
                    tr = P.tick(nc.tensor.transpose(
                        psA[:, ap * 512:ap * 512 + 128],
                        Mx[:, tg * 128:(tg + 1) * 128], idnf[:]), "te")
                    P.wdeps(nc.vector, [tr])
                    nc.vector.reduce_max(syc[:, tg:tg + 1],
                                         psA[:, ap * 512:ap * 512 + 128], axis=AX.X)
                    a_free[ap] = P.tick(nc.vector.drain(), "ve")
                for tg in range(NTG):
                    nc.vector.tensor_scalar(syc[:, 16 + tg:17 + tg], syc[:, tg:tg + 1],
                                            NRM, EPS, AL.mult, AL.max)
                    nc.vector.drain()
                    nc.vector.tensor_scalar(syc[:, 32 + tg:33 + tg],
                                            syc[:, 16 + tg:17 + tg], 1.0 / 128.0,
                                            None, AL.mult)
                    nc.vector.drain()
                    nc.vector.reciprocal(syc[:, 48 + tg:49 + tg],
                                         syc[:, 16 + tg:17 + tg])
                    nc.vector.drain()
                scdone = P.tick(nc.vector.engine_nop(), "ve")
                P.wdeps(nc.tensor, [scdone])
                for tg in range(NTG):
                    for half, col in ((0, 48), (1, 32)):
                        ap = tg % 2
                        P.wdeps(nc.tensor, [a_free[ap]] if a_free[ap] else [])
                        tr = P.tick(nc.tensor.transpose(
                            psA[0:1, ap * 512:ap * 512 + 128],
                            syc[:, col + tg:col + tg + 1], idnf[:]), "te")
                        P.wdeps(nc.vector, [tr])
                        nc.vector.tensor_copy(
                            syr[0:1, half * TT + tg * 128:half * TT + (tg + 1) * 128],
                            psA[0:1, ap * 512:ap * 512 + 128])
                        a_free[ap] = P.tick(nc.vector.drain(), "ve")
                rdone = P.mark("ve")
                P.wdeps(nc.tensor, [rdone])
                for half in range(2):
                    for sl in range(TT // 512):
                        ap = sl % 2
                        P.wdeps(nc.tensor, [a_free[ap]] if a_free[ap] else [])
                        bm = P.tick(nc.tensor.matmul(
                            psA[:, ap * 512:(ap + 1) * 512], ones1[:],
                            syr[0:1, half * TT + sl * 512:half * TT + (sl + 1) * 512],
                            start=True, stop=True), "te")
                        P.wdeps(nc.vector, [bm])
                        nc.vector.tensor_copy(
                            Sb[:, half * TT + sl * 512:half * TT + (sl + 1) * 512],
                            psA[:, ap * 512:(ap + 1) * 512])
                        a_free[ap] = P.tick(nc.vector.drain(), "ve")
            if dbg and tt == 0:
                P.wdeps(nc.gpsimd, [hdone_ae, P.mark("ve")])
                for u in range(NI):
                    P.dma(nc.gpsimd, dbg_h[:, u * TT:(u + 1) * TT],
                          bufA[:, u * TT:(u + 1) * TT], "wc")
                P.dma(nc.gpsimd, dbg_m[:, 0:TT], Mx[:, 0:TT], "wc")
                P.dma(nc.gpsimd, dbg_m[:, TT:2 * TT], Sb[:, 0:TT], "wc")
                P.dma(nc.gpsimd, dbg_m[:, 2 * TT:3 * TT], Sb[:, TT:2 * TT], "wc")
                dhd = P.mark("wc")
                P.wdeps(nc.vector, [dhd])
            P.wdeps(nc.vector, [hdone_ae])
            qdone_u = []
            for u in range(NI):
                nc.vector.scalar_tensor_tensor(qtm[:, 0:TT], bufA[:, u * TT:(u + 1) * TT],
                                               128.0 * NRM, Sb[:, 0:TT], AL.mult, AL.mult)
                nc.vector.drain()
                nc.vector.tensor_scalar(qi2[:], qtm[:, 0:TT], 1024.0, None, AL.add)
                nc.vector.drain()
                nc.vector.scalar_tensor_tensor(qd2[:], qtm[:, 0:TT], 1024.0, qi2[:],
                                               AL.add, AL.subtract)
                nc.vector.drain()
                nc.vector.tensor_scalar(qc3[:], qd2[:], 0.5, None, AL.is_gt)
                nc.vector.drain()
                nc.vector.tensor_scalar(qc4[:], qd2[:], 1.5, None, AL.is_gt)
                nc.vector.drain()
                nc.vector.tensor_scalar(qd2[:], qd2[:], -0.5, None, AL.is_lt)
                nc.vector.drain()
                nc.vector.tensor_sub(qc4[:], qc4[:], qd2[:])
                nc.vector.drain()
                nc.vector.scalar_tensor_tensor(qtm[:, 0:TT], qi2[:], -1024.0, qc3[:],
                                               AL.add, AL.add)
                nc.vector.drain()
                nc.vector.tensor_add(qtm[:, 0:TT], qtm[:, 0:TT], qc4[:])
                nc.vector.drain()
                nc.vector.tensor_scalar(qtm[:, 0:TT], qtm[:, 0:TT], -128.0, 127.0,
                                        AL.max, AL.min)
                nc.vector.drain()
                nc.vector.scalar_tensor_tensor(bufA[:, u * TT:(u + 1) * TT], qtm[:, 0:TT],
                                               mbar[:, 2:3], Sb[:, TT:2 * TT],
                                               AL.mult, AL.mult)
                qdone_u.append(P.tick(nc.vector.drain(), "ve"))
            qdone = qdone_u[-1]
            if dbg and tt == 0:
                P.wdeps(nc.gpsimd, [qdone])
                for u in range(NI):
                    P.dma(nc.gpsimd, dbg_yq[:, u * TT:(u + 1) * TT],
                          bufA[:, u * TT:(u + 1) * TT], "wc")
                qdone = P.mark("wc")
                P.wdeps(nc.vector, [qdone])
                qdone = P.tick(nc.vector.engine_nop(), "ve")
            with nc.psum_tensor(f"psD{tt}", [128, 8 * 512], F32) as psD:
                st = {"d_free": (), "atl_free": atl_free}

                def f2b_mms(th, eh, qdone, ald):
                    P.wdeps(nc.scalar, [st["atl_free"]])
                    for ec in range(8):
                        P.dma(nc.scalar, atl[:, ec * 512:(ec + 1) * 512],
                              att_s[ec * 128:(ec + 1) * 128,
                                    tof + th * 512:tof + (th + 1) * 512], "kb")
                    ald = P.mark("kb")
                    ald2 = ()
                    sl_ready = [P.dmat(nc.sync, wdsl[:, 0:1024],
                                       wds[eh * 1024:(eh + 1) * 1024, 0:128], "ws",
                                       deps=[wfree[0]]),
                                P.dmat(nc.sync, wdsl[:, 1024:2048],
                                       wds[eh * 1024:(eh + 1) * 1024, 128:256], "wt",
                                       deps=[wfree[1]])]
                    P.wdeps(nc.tensor, [st["d_free"]])
                    for u in range(NI + NE):
                        bs = u % 3
                        P.wdeps(nc.tensor, [sl_ready[u]])
                        if u < NI:
                            P.wdeps(nc.tensor, [qdone_u[u]])
                        elif u == NI:
                            P.wdeps(nc.tensor, [ald])
                        elif u == NI + 8:
                            P.wdeps(nc.tensor, [ald2])
                        mi = None
                        for tg in range(4):
                            for es in range(2):
                                if u < NI:
                                    lhs = bufA[:, u * TT + th * 512 + tg * 128:
                                               u * TT + th * 512 + (tg + 1) * 128]
                                else:
                                    ecl = (u - NI) % 8
                                    lhs = atl[:, ecl * 512 + tg * 128:
                                              ecl * 512 + (tg + 1) * 128]
                                mi = nc.tensor.matmul(
                                    psD[:, (tg * 2 + es) * 512:(tg * 2 + es + 1) * 512],
                                    lhs,
                                    wdsl[:, bs * 1024 + es * 512:bs * 1024 + (es + 1) * 512],
                                    start=(u == 0), stop=(u == NI + NE - 1))
                        wfree[bs] = P.tick(mi, "te")
                        if u == NI + 7:
                            for ec in range(8, NE):
                                P.dma(nc.scalar, atl[:, (ec - 8) * 512:(ec - 7) * 512],
                                      att_s[ec * 128:(ec + 1) * 128,
                                            tof + th * 512:tof + (th + 1) * 512], "kb",
                                      deps=[wfree[bs]] if ec == 8 else ())
                            ald2 = P.mark("kb")
                        if u + 2 < NI + NE:
                            nb = (u + 2) % 3
                            nname = ("ws", "wt", "wu")[nb]
                            if u + 2 < NI:
                                srcw = wds[eh * 1024:(eh + 1) * 1024,
                                           (u + 2) * 128:(u + 3) * 128]
                            else:
                                ec2 = u + 2 - NI
                                srcw = wout_bf[eh * 1024:(eh + 1) * 1024,
                                               ec2 * 128:(ec2 + 1) * 128]
                            P.dmat(nc.sync, wdsl[:, nb * 1024:(nb + 1) * 1024],
                                   srcw, nname, deps=[wfree[nb]])
                            sl_ready.append(P.mark(nname))
                    return P.mark("te")

                def f2b_evac(th, eh, dmm, tof):
                    for tg in range(4):
                        for es in range(2):
                            bp = es % 2
                            gr = HALO + tof + th * 512 + tg * 128
                            xl = P.dma(nc.scalar, xr[:, bp * 512:(bp + 1) * 512],
                                       xc[gr:gr + 128,
                                          eh * 1024 + es * 512:eh * 1024 + (es + 1) * 512],
                                       "xa" if bp == 0 else "xb",
                                       deps=[x_free[bp]])
                            P.wdeps(nc.vector, [dmm, xl, f_free[bp]])
                            nc.vector.tensor_add(
                                fo[:, bp * 512:(bp + 1) * 512],
                                psD[:, (tg * 2 + es) * 512:(tg * 2 + es + 1) * 512],
                                xr[:, bp * 512:(bp + 1) * 512])
                            x_free[bp] = fv = P.tick(nc.vector.drain(), "ve")
                            f_free[bp] = P.dma(
                                nc.sync,
                                out_c[tof + th * 512 + tg * 128:tof + th * 512 + (tg + 1) * 128,
                                      eh * 1024 + es * 512:eh * 1024 + (es + 1) * 512],
                                fo[:, bp * 512:(bp + 1) * 512],
                                "so" if bp == 0 else "sp", deps=[fv])
                    st["d_free"] = P.mark("ve")

                for th in range(TT // 512):
                    for eh in range(2):
                        dmm = f2b_mms(th, eh, qdone, ())
                        f2b_evac(th, eh, dmm, tof)
                    st["atl_free"] = P.mark("te")
                atl_free = st["atl_free"]
            buf_free = P.mark("te")
    for s, eng in P.dmaeng.items():
        if P.cnt[s] > 0:
            eng.wait_ge(P.sem[s], P.cnt[s])
    P.close()
    return nc


def _host_inputs(x, T):
    B, S, _ = x.shape
    percore = []
    npc = S // T
    for c in range(8):
        b, pos = c // npc, (c % npc) * T
        xcv = np.zeros((T + HALO, E), np.float32)
        lo = pos - HALO
        if lo < 0:
            xcv[HALO:, :] = x[b, pos:pos + T]
        else:
            xcv[:, :] = x[b, lo:pos + T]
        mz = np.zeros((4, 128, 640), np.float32)
        ppv = np.arange(128)[:, None]
        for s in range(4):
            jlv = np.arange(640)[None, :]
            ok = (jlv >= ppv) & (jlv <= ppv + 512)
            if pos == 0:
                ok &= (s * 128 + jlv) >= HALO
            mz[s] = np.where(ok, 0.0, NEGM)
        percore.append({"xc": xcv, "maskz": mz})
    return percore


def kernel(L0_attn_x, L0_attn_w_qkv, L0_attn_w_out,
           L0_ffn_w_gate, L0_ffn_w_up, L0_ffn_w_down):
    x = np.ascontiguousarray(np.asarray(L0_attn_x, dtype=np.float32))
    B, S, _ = x.shape
    T = (B * S) // 8
    nc = build_nc(T=T)
    w = {
        "w_qkv": np.ascontiguousarray(L0_attn_w_qkv, dtype=np.float32),
        "w_out": np.ascontiguousarray(L0_attn_w_out, dtype=np.float32),
        "w_gate": np.ascontiguousarray(L0_ffn_w_gate, dtype=np.float32),
        "w_up": np.ascontiguousarray(L0_ffn_w_up, dtype=np.float32),
        "w_down": np.ascontiguousarray(L0_ffn_w_down, dtype=np.float32),
    }
    in_maps = []
    for pc in _host_inputs(x, T):
        m = dict(w)
        m.update(pc)
        in_maps.append(m)
    res = run_bass_kernel_spmd(nc, in_maps, core_ids=list(range(8)))
    out = np.empty((B, S, E), np.float32)
    npc = S // T
    for c in range(8):
        b, pos = c // npc, (c % npc) * T
        out[b, pos:pos + T] = res.results[c]["out_c"]
    return out
